# version 17
# baseline (speedup 1.0000x reference)
import numpy as np
import concourse.bass as bass
import concourse.mybir as mybir

F32 = mybir.dt.float32
BF16 = mybir.dt.bfloat16
AX = mybir.AxisListType.X
ALU = mybir.AluOpType
ACT = mybir.ActivationFunctionType

T, D, H, I2, INTER = 2048, 2048, 16, 11264, 5632
NCH, NT, NI = 16, 16, 11
NI2 = 44
EPS = 1e-5
ISQD = 1.0 / float(np.sqrt(np.float64(2048.0)))
KB = 1024
SBUF_BASE = 16512
SBUF_END = 229344


ALLP = ("p0", "p1", "p1b", "gw", "p2", "p3", "p4", "dw", "p4b", "p5")


class Sched:
    ENGS = ("tensor", "vector", "scalar", "gpsimd", "sync")

    def __init__(self, nc):
        self.nc = nc
        self.q = {e: [] for e in self.ENGS}
        self.semnames = []
        self.count = {}

    def sem(self, name):
        self.semnames.append(name)
        self.count[name] = 0

    def wait(self, eng, *toks):
        best = {}
        for t in toks:
            if t is None:
                continue
            s, v = t
            if best.get(s, -1) < v:
                best[s] = v
        for s, v in best.items():
            self.q[eng].append(("wait", s, v))

    def op(self, eng, fn, inc=None, incby=1):
        tok = None
        if inc is not None:
            self.count[inc] += incby
            tok = (inc, self.count[inc])
        self.q[eng].append(("op", fn, inc, incby))
        return tok

    def dma(self, eng, out, in_, slot, accum=None):
        name = "d:" + slot
        if name not in self.count:
            self.sem(name)

        def fn(e, out=out, in_=in_, accum=accum):
            kw = {"accum_op": accum} if accum is not None else {}
            return e.dma_start(out=out, in_=in_, **kw)
        return self.op(eng, fn, inc=name, incby=16)

    def barrier(self):
        toks = [(s, self.count[s]) for s in self.semnames if self.count[s] > 0]
        for e in self.ENGS:
            self.wait(e, *toks)

    def build(self):
        nc = self.nc
        ctxs = [nc.semaphore(n) for n in self.semnames]
        sems = {}
        for n, cm in zip(self.semnames, ctxs):
            sems[n] = cm.__enter__()
        with nc.Block() as block:
            for eng in self.ENGS:
                items = self.q[eng]
                if not items:
                    continue

                def make_body(items=items):
                    def body(e):
                        for it in items:
                            if it[0] == "wait":
                                e.wait_ge(sems[it[1]], it[2])
                            else:
                                _, fn, inc, incby = it
                                ins = fn(e)
                                if inc is not None:
                                    ins.then_inc(sems[inc], incby)
                    return body

                getattr(block, eng)(make_body())
        return nc


def host_tables():
    import jax
    import jax.numpy as jnp
    with jax.default_device(jax.devices("cpu")[0]):
        inv_freq = 1.0 / 10000.0 ** (jnp.arange(0, D, 2, dtype=jnp.float32) / D)
        tt = jnp.arange(T, dtype=jnp.float32)
        ang = jnp.outer(tt, inv_freq)
        cos = np.asarray(jnp.cos(ang), dtype=np.float32)
        sin = np.asarray(jnp.sin(ang), dtype=np.float32)
    ccP = np.empty((D, T), dtype=np.float32)
    ssP = np.empty((D, T), dtype=np.float32)
    for h in range(H):
        pr = cos.T[64 * h:64 * (h + 1), :]
        si = sin.T[64 * h:64 * (h + 1), :]
        ccP[128 * h:128 * h + 64] = pr
        ccP[128 * h + 64:128 * h + 128] = pr
        ssP[128 * h:128 * h + 64] = -si
        ssP[128 * h + 64:128 * h + 128] = si
    return dict(
        ccP=ccP, ssP=ssP,
        ident=np.eye(128, dtype=np.float32),
        ones128=np.ones((128, 128), dtype=np.float32),
    )


def build_program(dbg=(), phases=None):
    if phases is None:
        phases = ALLP
    nc = bass.Bass("TRN2", target_bir_lowering=False,
                   detect_race_conditions=False)

    def dram(name, shape, dtype, kind=None):
        if kind is None:
            kind = "ExternalOutput" if name in dbg else "Internal"
        return nc.dram_tensor(name, list(shape), dtype, kind=kind).ap()

    x_h = dram("x", [T, D], F32, kind="ExternalInput")
    w_h = {k: dram(k, [D, D], F32, kind="ExternalInput")
           for k in ("wq", "wk", "wv", "wo")}
    gw_h = dram("gw", [I2, D], F32, kind="ExternalInput")
    dwm_h = dram("dwm", [D, INTER], F32, kind="ExternalInput")
    g1_h = dram("g1", [1, D], F32, kind="ExternalInput")
    g2_h = dram("g2", [1, INTER], F32, kind="ExternalInput")
    cc2_h = dram("cc2", [D, T], F32, kind="ExternalInput")
    ss2_h = dram("ss2", [D, T], F32, kind="ExternalInput")
    ident_h = dram("identc", [128, 128], F32, kind="ExternalInput")
    ones_h = dram("onesc", [128, 128], F32, kind="ExternalInput")
    out_h = dram("out", [T, D], F32, kind="ExternalOutput")

    sq_h = dram("sp_q", [D, T], F32)
    sk_h = dram("sp_k", [D, T], F32)
    svT_h = dram("sp_vT", [D, T], F32)
    sv_h = dram("sp_v", [T, D], F32)
    scth_h = dram("sp_cth", [D, T], BF16)
    sctl_h = dram("sp_ctl", [D, T], BF16)
    sx1n_h = dram("sp_x1n", [T, D], F32)
    sxqT_h = dram("sp_xqT", [D, T], BF16)
    sh_h = dram("sp_h", [T, INTER], F32)
    shqT_h = dram("sp_hqT", [INTER, T], BF16)
    sgwT_h = dram("sp_gwT", [D, I2], BF16)
    sdwT_h = dram("sp_dwT", [INTER, D], BF16)
    dbgsm_h = dram("dbg_sm", [128, 64], F32)
    dbgx1_h = dram("dbg_x1", [T, D], F32)

    S = Sched(nc)
    for s in ("sT", "sV", "sS"):
        S.sem(s)

    def buf(name, shape, dtype, off_kb, phase):
        off = SBUF_BASE + int(off_kb * KB)
        nbytes = int(np.prod(shape[1:])) * (4 if dtype == F32 else 2)
        assert off % 32 == 0 and off + nbytes <= SBUF_END, (name, off_kb, nbytes)
        return nc.alloc_sbuf_tensor_at(name, list(shape), dtype, offset=off).ap()

    G1bc = buf("c_g1bc", [128, D], F32, 196, "all")
    ident = buf("c_ident", [128, 128], F32, 204, "all")
    ones128 = buf("c_ones", [128, 128], F32, 204.5, "all")
    sm = buf("c_sm", [128, 64], F32, 205, "all")
    invsx1 = buf("c_invsx1", [128, 16], F32, 205.25, "all")
    invsx2 = buf("c_invsx2", [128, 16], F32, 205.3125, "all")
    lnc = buf("c_ln", [128, 32], F32, 205.375, "all")
    hss = buf("c_hss", [128, 192], F32, 205.5, "all")
    hmx = buf("c_hmx", [128, 176], F32, 206.25, "all")

    PS = nc.alloc_psum_tensor("PS", [128, 4096], F32).ap()

    def bank(i, w=512, o=0):
        return PS[:, 512 * i + o: 512 * i + o + w]

    t_cid = S.dma("sync", ident, ident_h, "c_id")
    t_con = S.dma("sync", ones128, ones_h, "c_on")
    S.op("vector", lambda e: e.memset(sm, 0.0))
    S.op("vector", lambda e: e.memset(lnc, 0.0))
    S.op("vector", lambda e: e.memset(invsx1, 0.0))
    S.op("vector", lambda e: e.memset(invsx2, 0.0))
    S.op("vector", lambda e: e.memset(sm[:, 4:5], -0.5))
    S.op("vector", lambda e: e.memset(sm[:, 5:6], 0.5))
    S.op("vector", lambda e: e.memset(sm[:, 9:10], EPS))
    S.op("vector", lambda e: e.memset(sm[:, 10:11], -128.0))
    S.op("vector", lambda e: e.memset(sm[:, 11:12], 128.5))
    S.op("vector", lambda e: e.memset(sm[:, 6:7], 0.0))
    S.op("vector", lambda e: e.memset(sm[:, 7:8], 0.0))
    S.op("vector", lambda e: e.memset(hmx, 0.0))
    t_minit = S.op("vector", lambda e: e.memset(hss, 0.0), inc="sV")

    scanb = buf("scanb", [128, 2048], F32, 188, "p01")
    scan_tiles = []
    if "gw" in phases or "dw" in phases or "p4" in phases or "p5" in phases:
        gw3 = gw_h.rearrange("(n p) c -> n p c", p=128)
        dw3 = dwm_h.rearrange("(n p) c -> n p c", p=128)
        for n in range(I2 // 128):
            scan_tiles.append(gw3[n])
        for n in range(NCH):
            scan_tiles.append(dw3[n, :, 0:2048])
            scan_tiles.append(dw3[n, :, 2048:4096])
            scan_tiles.append(dw3[n, :, 4096:5632])
    scan_i = [0]
    N_GW_TILES = I2 // 128

    def emit_scan(nt):
        for _ in range(nt):
            i = scan_i[0]
            if i >= len(scan_tiles):
                return
            if i == 0:
                S.wait("scalar", t_minit)
            scan_i[0] += 1
            src = scan_tiles[i]
            w = src.shape[-1]
            tok = S.dma("scalar", scanb[:, 0:w], src, "scanb")
            S.wait("scalar", tok)
            S.op("scalar", lambda e, w=w, i=i: e.activation(
                scanb[:, 0:w], scanb[:, 0:w], ACT.Abs,
                accum_out=hss[:, i:i + 1]), inc="sS")

    xTh = buf("xTh", [128, NCH * T], BF16, 0, "p01")
    xTl = buf("xTl", [128, NCH * T], BF16, 64, "p01")
    xld = [buf(f"xld{i}", [128, D], F32, 128 + 8 * i, "p0") for i in range(2)]

    if "p0" in phases:
        S.wait("tensor", t_cid, t_con)
        tT = [None, None]
        tEv = [None, None]
        tEs = [None, None]
        for tc in range(NT):
            b = tc % 2
            S.wait("sync", tT[b])
            tload = S.dma("sync", xld[b], x_h[128 * tc:128 * (tc + 1), :], f"xld{b}")
            S.wait("tensor", tload, tEv[b], tEs[b])
            for c in range(NCH):
                pb = PS[:, 2048 * b + 128 * c: 2048 * b + 128 * (c + 1)]
                tT[b] = S.op("tensor",
                             lambda e, pb=pb, b=b, c=c: e.transpose(
                                 pb, xld[b][:, 128 * c:128 * (c + 1)], ident),
                             inc="sT" if c == NCH - 1 else None)
            S.wait("scalar", tT[b])
            for k in range(4):
                for j in range(4):
                    c = 4 * k + j
                    src = bank(4 * b + k)[:, 128 * j:128 * (j + 1)]
                    dh = xTh[:, 2048 * c + 128 * tc: 2048 * c + 128 * (tc + 1)]
                    tEs[b] = S.op("scalar", lambda e, d=dh, s=src: e.copy(d, s),
                                  inc="sS" if (k == 3 and j == 3) else None)
            S.wait("vector", tT[b], tEs[b])
            for k in range(4):
                for j in range(4):
                    c = 4 * k + j
                    src = bank(4 * b + k)[:, 128 * j:128 * (j + 1)]
                    dh = xTh[:, 2048 * c + 128 * tc: 2048 * c + 128 * (tc + 1)]
                    dl = xTl[:, 2048 * c + 128 * tc: 2048 * c + 128 * (tc + 1)]
                    tEv[b] = S.op("vector",
                                  lambda e, dl=dl, s=src, dh=dh:
                                  e.tensor_sub(dl, s, dh),
                                  inc="sV" if (k == 3 and j == 3) else None)
            emit_scan(2)

    S.barrier()

    wld = [buf(f"wld{i}", [128, D], F32, 128 + 8 * i, "p1") for i in range(2)]
    wTh = [buf(f"wTh{i}", [128, D], BF16, 144 + 4 * i, "p1") for i in range(2)]
    wTl = [buf(f"wTl{i}", [128, D], BF16, 152 + 4 * i, "p1") for i in range(2)]
    ccb = buf("ccb", [128, T], F32, 160, "p1")
    ssb = buf("ssb", [128, T], F32, 168, "p1")
    qst = buf("qst", [128, T], F32, 176, "p1")
    m1b = buf("m1b", [128, 512], F32, 184, "p1")
    m2b = buf("m2b", [128, 512], F32, 186, "p1")

    if "p1" in phases:
        tTw = None
        tMM = None
        tRope = None
        tRopePrev = {0: None, 1: None}
        tSpill = None
        tTabs0 = tTabs1 = None
        wsched = []
        for wname, spill in (("wq", sq_h), ("wk", sk_h), ("wv", svT_h)):
            for fc in range(NCH):
                wsched.append((wname, spill, fc))
        wprep = {}

        def emit_wprep(i):
            nonlocal tTw
            if i >= len(wsched) or i in wprep:
                return
            wname, _sp, fc = wsched[i]
            b = i % 2
            rope = wname != "wv"
            wsrc = w_h[wname]
            S.wait("sync", tTw, *(wprep.get(i - 2, (None, None))))
            if rope:
                srcw = wsrc[128 * fc:128 * (fc + 1), :].rearrange(
                    "(p two) c -> two p c", two=2)
            else:
                srcw = wsrc[128 * fc:128 * (fc + 1), :]
            tld = S.dma("sync", wld[b], srcw, f"wld{b}")
            S.wait("tensor", tld, *(wprep.get(i - 2, (None, None))),
                   *(wprep.get(i - 1, (None, None))))
            tTwl = None
            for c in range(NCH):
                pb = PS[:, 2048 + 128 * c: 2048 + 128 * (c + 1)]
                tTwl = S.op("tensor",
                            lambda e, pb=pb, b=b, c=c: e.transpose(
                                pb, wld[b][:, 128 * c:128 * (c + 1)], ident),
                            inc="sT" if c == NCH - 1 else None)
            tTw = tTwl
            S.wait("scalar", tTwl)
            tEsW = None
            for k in range(4):
                tEsW = S.op("scalar",
                            lambda e, b=b, k=k: e.copy(
                                wTh[b][:, 512 * k:512 * (k + 1)], bank(4 + k)),
                            inc="sS" if k == 3 else None)
            S.wait("vector", tEsW)
            tEvW = None
            for k in range(4):
                tEvW = S.op("vector",
                            lambda e, b=b, k=k: e.tensor_sub(
                                wTl[b][:, 512 * k:512 * (k + 1)], bank(4 + k),
                                wTh[b][:, 512 * k:512 * (k + 1)]),
                            inc="sV" if k == 3 else None)
            wprep[i] = (tEvW, tEsW)

        emit_wprep(0)
        for i in range(len(wsched)):
            wname, spill, fc = wsched[i]
            b = i % 2
            rope = wname != "wv"
            if True:
                tEvW, tEsW = wprep[i]
                if rope:
                    S.wait("sync", tRope)
                    tTabs0 = S.dma("sync", ccb,
                                   cc2_h[128 * fc:128 * (fc + 1), :], "ccb")
                    tTabs1 = S.dma("sync", ssb,
                                   ss2_h[128 * fc:128 * (fc + 1), :], "ssb")
                tRopeP = {0: tRopePrev[0], 1: tRopePrev[1]}
                for pair in range(2):
                    if pair == 1:
                        emit_wprep(i + 1)
                    S.wait("tensor", tEvW, tEsW, tRopePrev[pair])
                    nmm = 0
                    for c in range(NCH):
                        for pi, (wt, xt) in enumerate(
                                ((wTh[b], xTh), (wTh[b], xTl), (wTl[b], xTh))):
                            for t4 in (2 * pair, 2 * pair + 1):
                                nmm += 1
                                first = (c == 0 and pi == 0)
                                last = (c == NCH - 1 and pi == 2)
                                tMM = S.op(
                                    "tensor",
                                    lambda e, wt=wt, xt=xt, c=c, t4=t4,
                                    first=first, last=last: e.matmul(
                                        bank(t4),
                                        wt[:, 128 * c:128 * (c + 1)],
                                        xt[:, 2048 * c + 512 * t4:
                                           2048 * c + 512 * (t4 + 1)],
                                        start=first, stop=last),
                                    inc="sT" if nmm == 96 else None)
                    if rope:
                        S.wait("vector", tMM, tSpill, tTabs0, tTabs1)
                        for t4 in (2 * pair, 2 * pair + 1):
                            p = bank(t4)
                            S.op("vector", lambda e, p=p, t4=t4: e.tensor_mul(
                                m1b, p, ccb[:, 512 * t4:512 * (t4 + 1)]))
                            S.op("vector", lambda e, p=p, t4=t4: e.tensor_mul(
                                m2b[0:64, :], p[64:128, :],
                                ssb[0:64, 512 * t4:512 * (t4 + 1)]))
                            S.op("vector", lambda e, p=p, t4=t4: e.tensor_mul(
                                m2b[64:128, :], p[0:64, :],
                                ssb[64:128, 512 * t4:512 * (t4 + 1)]))
                            tRopeP[pair] = S.op(
                                "vector", lambda e, t4=t4: e.tensor_add(
                                    qst[:, 512 * t4:512 * (t4 + 1)], m1b, m2b),
                                inc="sV" if t4 % 2 == 1 else None)
                    else:
                        S.wait("scalar", tMM, tSpill)
                        for t4 in (2 * pair, 2 * pair + 1):
                            tRopeP[pair] = S.op(
                                "scalar", lambda e, t4=t4: e.copy(
                                    qst[:, 512 * t4:512 * (t4 + 1)], bank(t4)),
                                inc="sS" if t4 % 2 == 1 else None)
                tRopePrev = tRopeP
                tRope = tRopeP[1]
                S.wait("gpsimd", tRopeP[0], tRopeP[1])
                tSpill = S.dma("gpsimd",
                               spill[128 * fc:128 * (fc + 1), :], qst, "qstsp")
                emit_scan(5)

    S.barrier()

    if "gw" in phases or "dw" in phases or "p4" in phases or "p5" in phases:
        emit_scan(len(scan_tiles))
        tlast = ("sS", S.count["sS"])
        S.wait("vector", tlast)
        S.op("vector", lambda e: e.reduce_sum(
            sm[:, 6:7], hss[:, 0:N_GW_TILES], AX))
        tlast = S.op("vector", lambda e: e.reduce_sum(
            sm[:, 7:8], hss[:, N_GW_TILES:N_GW_TILES + 48], AX), inc="sV")
        S.wait("tensor", tlast)
        S.op("tensor", lambda e: e.matmul(bank(0, 1), ones128, sm[:, 6:7],
                                          start=True, stop=True))
        tm = S.op("tensor", lambda e: e.matmul(bank(0, 1, o=4), ones128,
                                               sm[:, 7:8],
                                               start=True, stop=True), inc="sT")
        S.wait("vector", tm)
        S.op("vector", lambda e: e.tensor_scalar(
            sm[:, 0:1], bank(0, 1), 1.0 / (I2 * D), 1e-5, ALU.mult, ALU.max))
        tf1 = S.op("vector", lambda e: e.tensor_scalar(
            sm[:, 1:2], bank(0, 1, o=4), 1.0 / (D * INTER), 1e-5,
            ALU.mult, ALU.max), inc="sV")
        S.wait("vector", tf1)
        S.op("vector", lambda e: e.reciprocal(sm[:, 2:3], sm[:, 0:1]))
        S.op("vector", lambda e: e.reciprocal(sm[:, 3:4], sm[:, 1:2]))
        S.op("vector", lambda e: e.tensor_scalar_mul(
            sm[:, 0:1], sm[:, 0:1], 0.5 / 127.0))
        tsmf = S.op("vector", lambda e: e.tensor_scalar_mul(
            sm[:, 1:2], sm[:, 1:2], 0.5 / 127.0), inc="sV")
        if "dbg_sm" in dbg:
            S.wait("gpsimd", tsmf)
            S.dma("gpsimd", dbgsm_h, sm, "dbgsm")
        S.barrier()

    dbgiv_h = dram("dbg_iv", [128, 16], F32)
    if "dbg_iv" in dbg:
        pass

    tzld = [buf(f"tzld{i}", [128, INTER], F32, 0 + 22 * i, "gwdw")
            for i in range(2)]
    tzs1 = buf("tzs1", [128, INTER], F32, 44, "gwdw")
    tzs2 = buf("tzs2", [128, INTER], F32, 66, "gwdw")
    tztern = buf("tztern", [128, INTER], F32, 88, "gwdw")
    tzst = [buf(f"tzst{i}", [128, INTER], BF16, 110 + 11 * i, "gwdw")
            for i in range(2)]

    def ternarize_gen(src3, n_chunks, width, scol, dst_h):
        nw = width // 128
        ngr = (nw + 3) // 4
        tld = [None, None]
        tSg = None
        tTn = None
        tTr_l = None
        tEvc = [None, None]
        tSp = [None, None]
        for n in range(n_chunks):
            b = n % 2
            S.wait("sync", tSg if n >= 2 else None)
            tld[b] = S.dma("sync", tzld[b][:, 0:width], src3[n], f"tzld{b}")
            S.wait("scalar", tld[b], tTn)
            S.op("scalar", lambda e, b=b, w=width, scol=scol: e.activation(
                tzs1[:, 0:w], tzld[b][:, 0:w], ACT.Sign,
                bias=sm[:, 4:5], scale=sm[:, scol:scol + 1]))
            tSg = S.op("scalar", lambda e, b=b, w=width, scol=scol: e.activation(
                tzs2[:, 0:w], tzld[b][:, 0:w], ACT.Sign,
                bias=sm[:, 5:6], scale=sm[:, scol:scol + 1]), inc="sS")
            S.wait("vector", tSg, tTr_l)
            tTn = S.op("vector", lambda e, w=width: e.tensor_add(
                tztern[:, 0:w], tzs1[:, 0:w], tzs2[:, 0:w]), inc="sV")
            for g in range(ngr):
                s = (n * ngr + g) % 2
                jmax = min(4, nw - 4 * g)
                S.wait("tensor", tTn, tEvc[s])
                for j in range(jmax):
                    c = 4 * g + j
                    pb = PS[:, 512 * (4 + s) + 128 * j:
                            512 * (4 + s) + 128 * (j + 1)]
                    tTr_l = S.op("tensor",
                                 lambda e, pb=pb, c=c: e.transpose(
                                     pb, tztern[:, 128 * c:128 * (c + 1)], ident),
                                 inc="sT" if j == jmax - 1 else None)
                S.wait("scalar", tTr_l, tSp[b] if g == 0 else None)
                tEvc[s] = S.op("scalar",
                               lambda e, s=s, g=g, b=b, jmax=jmax: e.copy(
                                   tzst[b][:, 512 * g:512 * g + 128 * jmax],
                                   bank(4 + s, 128 * jmax)),
                               inc="sS")
            S.wait("gpsimd", tEvc[0], tEvc[1])
            dst = dst_h[:, 128 * n:128 * (n + 1)].rearrange(
                "(cc p) i -> p cc i", p=128)
            tSp[b] = S.dma("gpsimd", dst, tzst[b][:, 0:nw * 128], f"tzst{b}")
            yield n

    def drain(gen):
        if gen is None:
            return
        for _ in gen:
            pass

    gw_gen = None
    if "gw" in phases:
        gw3b = gw_h.rearrange("(n p) c -> n p c", p=128)
        gw_gen = ternarize_gen(gw3b, I2 // 128, D, 2, sgwT_h)

    vld = [buf(f"vld{i}", [128, D], F32, 128 + 8 * i, "p1b") for i in range(2)]
    vstg = [buf(f"vstg{i}", [128, 512], F32, 144 + 2 * i, "p1b") for i in range(2)]

    if "p1b" in phases:
        tTv = None
        tTvPrev = [None, None]
        tEvac = [None, None]
        tSpl = [None, None]
        for fc in range(NCH):
            b = fc % 2
            S.wait("sync", tTvPrev[b])
            tld = S.dma("sync", vld[b], svT_h[128 * fc:128 * (fc + 1), :], f"vld{b}")
            S.wait("tensor", tld)
            for g in range(4):
                s = (fc * 4 + g) % 2
                S.wait("tensor", tEvac[s])
                tTv = None
                for j in range(4):
                    ts_ = 4 * g + j
                    pb = PS[:, 2048 * s + 128 * j: 2048 * s + 128 * (j + 1)]
                    tTv = S.op("tensor",
                               lambda e, pb=pb, b=b, ts_=ts_: e.transpose(
                                   pb, vld[b][:, 128 * ts_:128 * (ts_ + 1)], ident),
                               inc="sT" if j == 3 else None)
                if g == 3:
                    tTvPrev[b] = tTv
                S.wait("scalar", tTv, tSpl[s])
                tEvac[s] = S.op("scalar",
                                lambda e, s=s: e.copy(vstg[s], bank(4 * s)),
                                inc="sS")
                S.wait("gpsimd", tEvac[s])
                dst = sv_h[512 * g:512 * (g + 1),
                           128 * fc:128 * (fc + 1)].rearrange(
                    "(j p) f -> p j f", p=128)
                tSpl[s] = S.dma("gpsimd", dst, vstg[s], f"vstg{s}")
            emit_scan(2)
            if gw_gen is not None:
                for _ in range(6):
                    next(gw_gen, None)

    drain(gw_gen)
    gw_gen = None
    S.barrier()

    qTb = [buf(f"qTb{i}", [128, T], F32, 0 + 8 * i, "p2") for i in range(2)]
    kTb = [buf(f"kTb{i}", [128, T], F32, 16 + 8 * i, "p2") for i in range(2)]
    vb = [buf(f"vb{i}", [128, T], F32, 32 + 8 * i, "p2") for i in range(2)]
    ebuf = buf("ebuf", [128, 16 * 1024], F32, 48, "p2")
    Eacc = buf("Eacc", [128, 1024], F32, 112, "p2")
    rbc = buf("rbc", [128, 1024], F32, 116, "p2")
    mctx = buf("mctx", [128, 1024], F32, 120, "p2")
    ctst = [buf(f"ctst{i}", [128, 2 * T], BF16, 124 + 8 * i, "p2")
            for i in range(2)]

    if "p2" in phases:
        tld_q = [None, None]
        tSc = None
        tExp = {}
        tEadd = None
        tFold = None
        tRec = None
        tCtx = None
        tCevK = None
        tCevS = None
        tSpC = [None, None]
        tQfree = [None, None]
        for h in range(H):
            b = h % 2
            S.wait("sync", tQfree[b])
            tq0 = S.dma("sync", qTb[b], sq_h[128 * h:128 * (h + 1), :], f"qTb{b}")
            tq1 = S.dma("sync", kTb[b], sk_h[128 * h:128 * (h + 1), :], f"kTb{b}")
            vsrc = sv_h[:, 128 * h:128 * (h + 1)].rearrange(
                "(kc p) f -> p kc f", p=128)
            tq2 = S.dma("sync", vb[b], vsrc, f"vb{b}")
            tld_q[b] = (tq0, tq1, tq2)
            for qbp in range(2):
                for kc in range(NCH):
                    pr = kc % 2
                    tprev = tExp.get(kc - 2) if kc >= 2 else tExp.get(14 + kc)
                    S.wait("tensor", *(tld_q[b] or (None,)), tprev, tFold)
                    for qh in range(2):
                        tSc = S.op(
                            "tensor",
                            lambda e, b=b, kc=kc, qh=qh, qbp=qbp, pr=pr: e.matmul(
                                bank(2 * pr + qh),
                                kTb[b][:, 128 * kc:128 * (kc + 1)],
                                qTb[b][:, 1024 * qbp + 512 * qh:
                                       1024 * qbp + 512 * (qh + 1)],
                                start=True, stop=True),
                            inc="sT" if qh == 1 else None)
                    S.wait("scalar", tSc)
                    for qh in range(2):
                        tExp[kc] = S.op(
                            "scalar",
                            lambda e, kc=kc, qh=qh, pr=pr: e.activation(
                                ebuf[:, 1024 * kc + 512 * qh:
                                     1024 * kc + 512 * (qh + 1)],
                                bank(2 * pr + qh), ACT.Exp, scale=ISQD),
                            inc="sS" if qh == 1 else None)
                    S.wait("vector", tExp[kc])
                    if kc == 0:
                        tEadd = S.op("vector", lambda e, kc=kc: e.tensor_copy(
                            Eacc, ebuf[:, 1024 * kc:1024 * (kc + 1)]),
                            inc="sV")
                    else:
                        tEadd = S.op("vector", lambda e, kc=kc: e.tensor_add(
                            Eacc, Eacc, ebuf[:, 1024 * kc:1024 * (kc + 1)]),
                            inc="sV" if kc == NCH - 1 else None)
                S.wait("tensor", tCevK, tCevS)
                for kc in range(NCH):
                    for qh in range(2):
                        tCtx = S.op(
                            "tensor",
                            lambda e, b=b, kc=kc, qh=qh, qbp=qbp: e.matmul(
                                bank(4 + qh),
                                vb[b][:, 128 * kc:128 * (kc + 1)],
                                ebuf[:, 1024 * kc + 512 * qh:
                                     1024 * kc + 512 * (qh + 1)],
                                start=(kc == 0), stop=(kc == NCH - 1)),
                            inc="sT" if (kc == NCH - 1 and qh == 1) else None)
                if qbp == 1:
                    tQfree[b] = tCtx
                S.wait("tensor", tEadd, tRec)
                for qh in range(2):
                    tFold = S.op("tensor", lambda e, qh=qh: e.matmul(
                        bank(6 + qh), ones128, Eacc[:, 512 * qh:512 * (qh + 1)],
                        start=True, stop=True),
                        inc="sT" if qh == 1 else None)
                S.wait("vector", tFold)
                tRec = S.op("vector", lambda e: e.reciprocal(
                    rbc, PS[:, 3072:4096]), inc="sV")
                S.wait("vector", tCtx, tRec, tSpC[b])
                tCevK = S.op("vector", lambda e: e.tensor_mul(
                    mctx, PS[:, 2048:3072], rbc), inc="sV")
                S.wait("scalar", tCevK)
                tCevS = S.op("scalar", lambda e, b=b, qbp=qbp: e.copy(
                    ctst[b][:, 1024 * qbp:1024 * (qbp + 1)], mctx), inc="sS")
                S.wait("vector", tCevS)
                tCevK = S.op("vector", lambda e, b=b, qbp=qbp: e.tensor_sub(
                    ctst[b][:, 2048 + 1024 * qbp:2048 + 1024 * (qbp + 1)],
                    mctx, ctst[b][:, 1024 * qbp:1024 * (qbp + 1)]), inc="sV")
            S.wait("gpsimd", tCevK, tCevS)
            S.dma("gpsimd", scth_h[128 * h:128 * (h + 1), :],
                  ctst[b][:, 0:2048], f"ctst{b}")
            tSpC[b] = S.dma("gpsimd", sctl_h[128 * h:128 * (h + 1), :],
                            ctst[b][:, 2048:4096], f"ctst{b}")

    S.barrier()

    woTh = buf("woTh", [128, NCH * T], BF16, 0, "p3")
    woTl = buf("woTl", [128, NCH * T], BF16, 64, "p3")
    wld2 = [buf(f"wld2{i}", [128, D], F32, 128 + 8 * i, "p3prep") for i in range(2)]
    ctile = buf("ctile", [128, D], F32, 128, "p3")
    xqb = buf("xqb", [128, D], F32, 136, "p3")
    xqstage = buf("xqstage", [128, D], BF16, 144, "p3")
    cthc = [buf(f"cthc{i}", [128, D], BF16, 148 + 4 * i, "p3") for i in range(2)]
    ctlc = [buf(f"ctlc{i}", [128, D], BF16, 156 + 4 * i, "p3") for i in range(2)]
    xch = [buf(f"xch{i}", [128, D], F32, 164 + 8 * i, "p3") for i in range(2)]
    xAt = buf("xAt", [128, D], F32, 180, "p3")
    xBt = buf("xBt", [128, D], F32, 188, "p3")

    def vsmall(fn):
        tok = S.op("vector", fn, inc="sV")
        S.wait("vector", tok)
        return tok

    def emit_ladder(eng, vtile, utile, cbt, w, uguard=None):
        w2 = w // 2
        halves = [(slice(0, w2),), (slice(w2, w),)]
        if uguard is None:
            uguard = ()
        elif not isinstance(uguard, tuple):
            uguard = (uguard,)
        tS = [None, None]
        tV = [list(uguard), list(uguard)]
        for bi, bv in enumerate((128.0, 64.0, 32.0, 16.0, 8.0, 4.0, 2.0, 1.0)):
            for hi, (sl,) in enumerate(halves):
                src_u = vtile if bi == 0 else utile
                S.wait("scalar", *tV[hi])
                tS[hi] = S.op("scalar", lambda e, s=src_u, sl=sl: e.activation(
                    cbt[:, sl], s[:, sl], ACT.Sign, bias=sm[:, 10:11]),
                    inc="sS")
                S.wait("vector", tS[hi])
                tV[hi] = [S.op("vector",
                               lambda e, s=src_u, sl=sl, bv=bv:
                               e.scalar_tensor_tensor(
                                   utile[:, sl], cbt[:, sl], -bv / 2.0,
                                   s[:, sl], ALU.mult, ALU.add), inc="sV")]

    if "p3" in phases:
        tTw = [None, None]
        tEv2 = [None, None]
        tEs2 = [None, None]
        for r in range(NCH):
            b = r % 2
            S.wait("sync", tTw[b])
            tld = S.dma("sync", wld2[b], w_h["wo"][128 * r:128 * (r + 1), :], f"wld2{b}")
            S.wait("tensor", tld, tEv2[b], tEs2[b])
            for c in range(NCH):
                pb = PS[:, 2048 * b + 128 * c: 2048 * b + 128 * (c + 1)]
                tTw[b] = S.op("tensor",
                              lambda e, pb=pb, b=b, c=c: e.transpose(
                                  pb, wld2[b][:, 128 * c:128 * (c + 1)], ident),
                              inc="sT" if c == NCH - 1 else None)
            S.wait("scalar", tTw[b])
            for k in range(4):
                for j in range(4):
                    c = 4 * k + j
                    srcp = bank(4 * b + k)[:, 128 * j:128 * (j + 1)]
                    dh = woTh[:, 2048 * c + 128 * r: 2048 * c + 128 * (r + 1)]
                    tEs2[b] = S.op("scalar", lambda e, d=dh, s=srcp: e.copy(d, s),
                                   inc="sS" if (k == 3 and j == 3) else None)
            S.wait("vector", tTw[b], tEs2[b])
            for k in range(4):
                for j in range(4):
                    c = 4 * k + j
                    srcp = bank(4 * b + k)[:, 128 * j:128 * (j + 1)]
                    dh = woTh[:, 2048 * c + 128 * r: 2048 * c + 128 * (r + 1)]
                    dl = woTl[:, 2048 * c + 128 * r: 2048 * c + 128 * (r + 1)]
                    tEv2[b] = S.op("vector",
                                   lambda e, dl=dl, s=srcp, dh=dh:
                                   e.tensor_sub(dl, s, dh),
                                   inc="sV" if (k == 3 and j == 3) else None)
        tg = S.dma("sync", G1bc[0:1, :], g1_h, "g1")
        S.wait("tensor", tg, tEv2[0], tEv2[1], tEs2[0], tEs2[1])
        tgb = None
        for j in range(4):
            tgb = S.op("tensor", lambda e, j=j: e.matmul(
                bank(j), ones128[0:1, :], G1bc[0:1, 512 * j:512 * (j + 1)],
                start=True, stop=True), inc="sT" if j == 3 else None)
        S.wait("scalar", tgb)
        tgs = None
        for j in range(4):
            tgs = S.op("scalar", lambda e, j=j: e.copy(
                G1bc[:, 512 * j:512 * (j + 1)], bank(j)),
                inc="sS" if j == 3 else None)

        tMMo = None
        tLd = [None, None]
        tFree = [None, None]
        tSpXQ = None
        tSpX1N = None
        tXqT = None
        tEvXq = tEv2[1]
        tAddPrev = {0: tgs, 1: tgs}
        tXqPend = [None]
        for tc in range(NT):
            b = tc % 2
            S.wait("sync", *(tFree[b] or (None,)))
            for m in range(4):
                csrc = scth_h[512 * m:512 * (m + 1),
                              128 * tc:128 * (tc + 1)].rearrange(
                    "(fc p) t -> p fc t", p=128)
                tc0 = S.dma("sync", cthc[b][:, 512 * m:512 * (m + 1)], csrc, f"cthc{b}")
            for m in range(4):
                csrc = sctl_h[512 * m:512 * (m + 1),
                              128 * tc:128 * (tc + 1)].rearrange(
                    "(fc p) t -> p fc t", p=128)
                tc1 = S.dma("sync", ctlc[b][:, 512 * m:512 * (m + 1)], csrc, f"ctlc{b}")
            tc2 = S.dma("sync", xch[b], x_h[128 * tc:128 * (tc + 1), :], f"xch{b}")
            tLd[b] = (tc0, tc1, tc2)
            tMMp = {}
            for pair in range(2):
                S.wait("tensor", *(tLd[b] or (None,)), tAddPrev[pair], tgs)
                nmm = 0
                for fc in range(NCH):
                    for pi, (ct, wt) in enumerate(
                            ((cthc[b], woTh), (cthc[b], woTl), (ctlc[b], woTh))):
                        for f4 in (2 * pair, 2 * pair + 1):
                            nmm += 1
                            tMMp[pair] = S.op(
                                "tensor",
                                lambda e, ct=ct, wt=wt, fc=fc, f4=f4,
                                first=(fc == 0 and pi == 0),
                                last=(fc == NCH - 1 and pi == 2): e.matmul(
                                    bank(f4),
                                    ct[:, 128 * fc:128 * (fc + 1)],
                                    wt[:, 2048 * fc + 512 * f4:
                                       2048 * fc + 512 * (f4 + 1)],
                                    start=first, stop=last),
                                inc="sT" if nmm == 96 else None)
            tMMo = tMMp[1]
            if tXqPend[0] is not None:
                tXqPend[0]()
                tXqPend[0] = None
            S.wait("vector", tMMp[0], tSpX1N)
            tadd0 = S.op("vector", lambda e, b=b: e.tensor_add(
                xAt[:, 0:1024], PS[:, 0:1024], xch[b][:, 0:1024]), inc="sV")
            S.wait("vector", tMMp[1])
            tadd = S.op("vector", lambda e, b=b: e.tensor_add(
                xAt[:, 1024:2048], PS[:, 1024:2048], xch[b][:, 1024:2048]),
                inc="sV")
            tAddPrev = {0: tadd0, 1: tadd}
            tFree[b] = (tMMo, tadd)
            if "dbg_x1" in dbg:
                S.wait("gpsimd", tadd)
                tdbg = S.dma("gpsimd", dbgx1_h[128 * tc:128 * (tc + 1), :],
                             xAt, "dbgx1")
                S.wait("vector", tdbg)
            vsmall(lambda e: e.reduce_sum(lnc[:, 0:1], xAt, AX))
            vsmall(lambda e: e.tensor_scalar_mul(
                lnc[:, 2:3], lnc[:, 0:1], -1.0 / 2048.0))
            S.op("vector", lambda e: e.tensor_scalar_add(
                xBt, xAt, lnc[:, 2:3]))
            S.op("vector", lambda e: e.tensor_mul(ctile, xBt, xBt))
            tV = S.op("vector", lambda e: e.reduce_sum(
                lnc[:, 4:5], ctile, AX), inc="sV")
            S.wait("scalar", tV)
            tSq2 = S.op("scalar", lambda e: e.activation(
                lnc[:, 5:6], lnc[:, 4:5], ACT.Sqrt, scale=1.0 / 2048.0,
                bias=sm[:, 9:10]), inc="sS")
            S.wait("vector", tSq2)
            vsmall(lambda e: e.reciprocal(lnc[:, 6:7], lnc[:, 5:6]))
            tX1n = S.op("vector", lambda e: e.tensor_scalar_mul(
                xBt, xBt, lnc[:, 6:7]), inc="sV")
            S.wait("gpsimd", tX1n)
            tSpX1N = S.dma("gpsimd", sx1n_h[128 * tc:128 * (tc + 1), :], xBt, "x1nsp")
            S.op("vector", lambda e: e.tensor_mul(ctile, xBt, xBt))
            tSq3 = S.op("vector", lambda e: e.reduce_sum(
                lnc[:, 7:8], ctile, AX), inc="sV")
            S.wait("scalar", tSq3)
            tRs = S.op("scalar", lambda e: e.activation(
                lnc[:, 8:9], lnc[:, 7:8], ACT.Sqrt, scale=1.0 / 2048.0,
                bias=sm[:, 9:10]), inc="sS")
            S.wait("vector", tRs)
            vsmall(lambda e: e.reciprocal(lnc[:, 9:10], lnc[:, 8:9]))
            tXn = S.op("vector", lambda e: e.scalar_tensor_tensor(
                xAt, xBt, lnc[:, 9:10], G1bc, ALU.mult, ALU.mult), inc="sV")
            vsmall(lambda e: e.reduce_max(
                lnc[:, 10:11], xAt, AX, apply_absolute_value=True))
            tAM = vsmall(lambda e: e.tensor_single_scalar(
                lnc[:, 10:11], lnc[:, 10:11], 1e-5, ALU.max))
            vsmall(lambda e: e.reciprocal(lnc[:, 11:12], lnc[:, 10:11]))
            tIS2 = vsmall(lambda e: e.tensor_scalar_mul(
                lnc[:, 11:12], lnc[:, 11:12], 127.0))
            S.wait("scalar", tAM)
            S.op("scalar", lambda e, tc=tc: e.activation(
                invsx1[:, tc:tc + 1], lnc[:, 10:11], ACT.Identity,
                scale=sm[:, 0:1]))
            tVq = S.op("vector", lambda e: e.tensor_scalar(
                xAt, xAt, lnc[:, 11:12], 128.5, ALU.mult, ALU.add), inc="sV")
            emit_ladder("vector", xAt, xBt, ctile, D, uguard=(tSpX1N, tVq))
            S.wait("vector", tXqT)
            tXq = S.op("vector", lambda e: e.scalar_tensor_tensor(
                xqb, xAt, -0.5, xBt, ALU.add, ALU.subtract), inc="sV")
            def emit_xq(tc=tc, tXq=tXq):
                nonlocal tXqT, tEvXq, tSpXQ
                S.wait("tensor", tXq, tEvXq, tEs2[1])
                for c in range(NCH):
                    pb = PS[:, 2048 + 128 * c:2048 + 128 * (c + 1)]
                    tXqT = S.op("tensor", lambda e, pb=pb, c=c: e.transpose(
                        pb, xqb[:, 128 * c:128 * (c + 1)], ident),
                        inc="sT" if c == NCH - 1 else None)
                S.wait("scalar", tXqT, tSpXQ)
                for k in range(4):
                    tEvXq = S.op("scalar", lambda e, k=k: e.copy(
                        xqstage[:, 512 * k:512 * (k + 1)], bank(4 + k)),
                        inc="sS" if k == 3 else None)
                S.wait("gpsimd", tEvXq)
                dstq = sxqT_h[:, 128 * tc:128 * (tc + 1)].rearrange(
                    "(cc p) t -> p cc t", p=128)
                tSpXQ = S.dma("gpsimd", dstq, xqstage, "xqsp")

            tXqPend[0] = emit_xq
        if tXqPend[0] is not None:
            tXqPend[0]()
            tXqPend[0] = None

    if "dbg_iv" in dbg and "p3" in phases:
        tiv = ("sS", S.count["sS"])
        S.wait("gpsimd", tiv, ("sV", S.count["sV"]))
        S.dma("gpsimd", dbgiv_h, invsx1, "dbgiv")
    S.barrier()

    xqT = buf("xqT", [128, NCH * T], BF16, 0, "p4")
    gwg = [buf(f"gwg{i}", [128, NCH * 512], BF16, 64 + 16 * i, "p4")
           for i in range(2)]
    gwu = [buf(f"gwu{i}", [128, NCH * 512], BF16, 96 + 16 * i, "p4")
           for i in range(2)]
    sgb = [buf(f"sgb{i}", [128, 512], F32, 128 + 2 * i, "p4") for i in range(2)]
    hstg = [buf(f"hstg{i}", [128, 512], F32, 132 + 2 * i, "p4") for i in range(2)]
    sqd = buf("sqd", [128, 512], F32, 136, "p4")
    G2bc4 = buf("G2bc4", [128, INTER], F32, 138, "p4")

    tzq = [buf(f"tzq{i}", [128, 1408], F32, 160 + 5.5 * i, "p4t")
           for i in range(2)]
    tzs1q = [buf(f"tzs1q{i}", [128, 1408], F32, 171 + 5.5 * i, "p4t")
             for i in range(2)]
    tzs2q = buf("tzs2q", [128, 1408], F32, 182, "p4t")
    tzstq = [buf(f"tzstq{i}", [128, 512], BF16, 187.5 + 1 * i, "p4t")
             for i in range(2)]

    def tern_dw_gen(tEvB2init):
        dw3b = dwm_h.rearrange("(n p) c -> n p c", p=128)
        tEvB2 = dict(tEvB2init)
        tSp2 = [None, None]
        tLd2 = [None, None]
        tSgLast = [None, None]
        tTrT = [None, None]
        tAddT = None
        step = 0
        for n in range(NCH):
            for q in range(4):
                sq = step % 2
                S.wait("sync", tSgLast[sq])
                tLd2[sq] = S.dma("sync", tzq[sq],
                                 dw3b[n][:, 1408 * q:1408 * (q + 1)],
                                 f"tzq{sq}")
                S.wait("scalar", tLd2[sq], tTrT[sq], tAddT)
                S.op("scalar", lambda e, sq=sq: e.activation(
                    tzs1q[sq], tzq[sq], ACT.Sign, bias=sm[:, 4:5],
                    scale=sm[:, 3:4]))
                tSg = S.op("scalar", lambda e, sq=sq: e.activation(
                    tzs2q, tzq[sq], ACT.Sign, bias=sm[:, 5:6],
                    scale=sm[:, 3:4]), inc="sS")
                tSgLast[sq] = tSg
                S.wait("vector", tSg)
                tAdd = S.op("vector", lambda e, sq=sq: e.tensor_add(
                    tzs1q[sq], tzs1q[sq], tzs2q), inc="sV")
                tAddT = tAdd
                tTr = None
                for sub in range(3):
                    jmax = 4 if sub < 2 else 3
                    bk = 2 + ((step * 3 + sub) % 2)
                    hb = (step * 3 + sub) % 2
                    S.wait("tensor", tAdd if sub == 0 else None,
                           tEvB2.get(bk))
                    for jj in range(jmax):
                        c = 4 * sub + jj
                        pb = PS[:, 512 * bk + 128 * jj:
                                512 * bk + 128 * (jj + 1)]
                        tTr = S.op("tensor", lambda e, pb=pb, sq=sq, c=c:
                                   e.transpose(
                                       pb, tzs1q[sq][:, 128 * c:128 * (c + 1)],
                                       ident),
                                   inc="sT" if jj == jmax - 1 else None)
                    S.wait("scalar", tTr, tSp2[hb])
                    tEvB2[bk] = S.op("scalar",
                                     lambda e, hb=hb, bk=bk, jmax=jmax:
                                     e.copy(tzstq[hb][:, 0:128 * jmax],
                                            bank(bk, 128 * jmax)), inc="sS")
                    S.wait("scalar", tEvB2[bk])
                    nr = 128 * jmax
                    r0 = 1408 * q + 512 * sub
                    dst = sdwT_h[r0:r0 + nr, 128 * n:128 * (n + 1)].rearrange(
                        "(cc p) i -> p cc i", p=128)
                    tSp2[hb] = S.dma("scalar", dst, tzstq[hb][:, 0:nr],
                                     f"tzspl{hb}")
                tTrT[sq] = tTr
                step += 1
                yield step

    if "p4" in phases:
        tg2 = S.dma("sync", G2bc4[0:1, :], g2_h, "g2")
        S.wait("tensor", tg2)
        tEvJ = {}
        for jj in range(NI):
            bb = 2 + jj % 2
            S.wait("tensor", tEvJ.get(jj - 2))
            tb2 = S.op("tensor", lambda e, jj=jj, bb=bb: e.matmul(
                bank(bb), ones128[0:1, :], G2bc4[0:1, 512 * jj:512 * (jj + 1)],
                start=True, stop=True), inc="sT")
            S.wait("scalar", tb2)
            tEvJ[jj] = S.op("scalar", lambda e, jj=jj, bb=bb: e.copy(
                G2bc4[:, 512 * jj:512 * (jj + 1)], bank(bb)), inc="sS")
        tG2rdy = tEvJ[NI - 1]
        tern_it = (tern_dw_gen({2: tEvJ[NI - 1], 3: tEvJ[NI - 2]})
                   if "dw" in phases else None)
        tern_cnt = [0]
        tXQ = None
        for c in range(NCH):
            tXQ = S.dma("sync", xqT[:, 2048 * c:2048 * (c + 1)],
                        sxqT_h[128 * c:128 * (c + 1), :], "xqTld")
        tGld = [None, None]
        tMg = {}
        tEvG = {}
        tSpH = [None, None]
        for j in range(NI):
            b = j % 2
            S.wait("sync", tMg.get((j - 2, NT - 1)))
            for m in range(4):
                gsrc = sgwT_h[512 * m:512 * (m + 1),
                              512 * j:512 * (j + 1)].rearrange(
                    "(cc p) i -> p cc i", p=128)
                tGg = S.dma("sync", gwg[b][:, 2048 * m:2048 * (m + 1)], gsrc, f"gwg{b}")
            for m in range(4):
                usrc = sgwT_h[512 * m:512 * (m + 1),
                              INTER + 512 * j:INTER + 512 * (j + 1)].rearrange(
                    "(cc p) i -> p cc i", p=128)
                tGu = S.dma("sync", gwu[b][:, 2048 * m:2048 * (m + 1)],
                                usrc, f"gwu{b}")
            tGld[b] = (tGg, tGu)
            for tc in range(NT):
                s = tc % 2
                gb, ub = 4 * s, 4 * s + 1
                S.wait("tensor", tXQ, *(tGld[b] or (None,)), tEvG.get((j, tc - 2)),
                       tEvG.get((j - 1, tc + 14)))
                for c in range(NCH):
                    off = 2048 * (c // 4) + 512 * (c % 4)
                    S.op("tensor", lambda e, c=c, tc=tc, b=b, gb=gb, off=off:
                         e.matmul(bank(gb),
                                  xqT[:, 2048 * c + 128 * tc:
                                      2048 * c + 128 * (tc + 1)],
                                  gwg[b][:, off:off + 512],
                                  start=(c == 0), stop=(c == NCH - 1)))
                    tMg[(j, tc)] = S.op(
                        "tensor", lambda e, c=c, tc=tc, b=b, ub=ub, off=off:
                        e.matmul(bank(ub),
                                 xqT[:, 2048 * c + 128 * tc:
                                     2048 * c + 128 * (tc + 1)],
                                 gwu[b][:, off:off + 512],
                                 start=(c == 0), stop=(c == NCH - 1)),
                        inc="sT" if c == NCH - 1 else None)
                S.wait("scalar", tMg[(j, tc)],
                       tEvG.get((j, tc - 2)), tEvG.get((j - 1, tc + 14)))
                tSg2 = S.op("scalar", lambda e, s=s, gb=gb, tc=tc: e.activation(
                    sgb[s], bank(gb), ACT.Silu, scale=invsx1[:, tc:tc + 1]),
                    inc="sS")
                S.wait("vector", tSg2, tSpH[s],
                       tG2rdy if (j == 0 and tc == 0) else None)
                tH = S.op("vector", lambda e, s=s, ub=ub, tc=tc:
                          e.scalar_tensor_tensor(
                              hstg[s], bank(ub), invsx1[:, tc:tc + 1], sgb[s],
                              ALU.mult, ALU.mult), inc="sV")
                S.op("vector", lambda e, s=s: e.tensor_mul(
                    sqd, hstg[s], hstg[s]))
                S.op("vector", lambda e, tc=tc, j=j: e.reduce_sum(
                    hss[:, 11 * tc + j:11 * tc + j + 1], sqd, AX))
                S.op("vector", lambda e, s=s, j=j: e.tensor_mul(
                    hstg[s], hstg[s], G2bc4[:, 512 * j:512 * (j + 1)]))
                tSqH = S.op("vector", lambda e, tc=tc, j=j, s=s: e.reduce_max(
                    hmx[:, 11 * tc + j:11 * tc + j + 1], hstg[s], AX,
                    apply_absolute_value=True), inc="sV")
                tEvG[(j, tc)] = tSqH
                S.wait("gpsimd", tSqH)
                tSpH[s] = S.dma("gpsimd",
                                sh_h[128 * tc:128 * (tc + 1),
                                     512 * j:512 * (j + 1)], hstg[s], f"hsp{s}")
                target = (64 * (j * NT + tc + 1)) // (NI * NT)
                while tern_it is not None and tern_cnt[0] < target:
                    if next(tern_it, None) is None:
                        tern_it = None
                        break
                    tern_cnt[0] += 1
        drain(tern_it)
        tern_it = None

    if "dw" in phases and "p4" not in phases:
        drain(tern_dw_gen({2: None, 3: None}))

    S.barrier()

    hldq = [buf(f"hldq{i}", [128, 1408], F32, 158 + 5.5 * i, "p5m")
            for i in range(2)]
    uq2 = [buf(f"uq{i}", [128, 1408], F32, 169 + 5.5 * i, "p5m")
           for i in range(2)]
    sgq = buf("sgq", [128, 1408], F32, 180, "p5m")
    hqstq = [buf(f"hqstq{i}", [128, 1408], BF16, 185.5 + 2.75 * i, "p5m")
             for i in range(2)]

    spHq = {}
    inv2tok = {}

    def p4b_gen():
        tTrU = [None, None]
        tEvB = {6: None, 7: None}
        tSpSt = [None, None]
        tFreeLd = [None, None]
        tPend = [None]
        qidx = 0
        for tc in range(NT):
            tSum2 = vsmall(lambda e, tc=tc: e.reduce_sum(
                lnc[:, 16:17], hss[:, 11 * tc:11 * (tc + 1)], AX))
            S.wait("scalar", tSum2)
            tRs2 = S.op("scalar", lambda e: e.activation(
                lnc[:, 17:18], lnc[:, 16:17], ACT.Sqrt, scale=1.0 / INTER,
                bias=sm[:, 9:10]), inc="sS")
            S.wait("vector", tRs2)
            vsmall(lambda e: e.reciprocal(lnc[:, 18:19], lnc[:, 17:18]))
            vsmall(lambda e, tc=tc: e.reduce_max(
                lnc[:, 19:20], hmx[:, 11 * tc:11 * (tc + 1)], AX,
                apply_absolute_value=True))
            vsmall(lambda e: e.tensor_scalar_mul(
                lnc[:, 20:21], lnc[:, 19:20], lnc[:, 18:19]))
            tClip = vsmall(lambda e: e.tensor_single_scalar(
                lnc[:, 20:21], lnc[:, 20:21], 1e-5, ALU.max))
            vsmall(lambda e: e.reciprocal(lnc[:, 21:22], lnc[:, 20:21]))
            vsmall(lambda e: e.tensor_scalar_mul(
                lnc[:, 21:22], lnc[:, 21:22], 127.0))
            vsmall(lambda e: e.tensor_scalar_mul(
                lnc[:, 22:23], lnc[:, 21:22], lnc[:, 18:19]))
            S.wait("scalar", tClip)
            inv2tok[tc] = S.op("scalar", lambda e, tc=tc: e.activation(
                invsx2[:, tc:tc + 1], lnc[:, 20:21], ACT.Identity,
                scale=sm[:, 1:2]), inc="sS")
            for q in range(4):
                b = qidx % 2
                if tPend[0] is not None:
                    tPend[0]()
                    tPend[0] = None
                S.wait("sync", tFreeLd[b])
                tLd = S.dma("sync", hldq[b],
                            sh_h[128 * tc:128 * (tc + 1),
                                 1408 * q:1408 * (q + 1)], f"hldq{b}")
                S.wait("vector", tLd)
                tV1 = S.op("vector", lambda e, b=b: e.tensor_scalar(
                    hldq[b], hldq[b], lnc[:, 22:23], 128.5,
                    ALU.mult, ALU.add), inc="sV")
                emit_ladder("vector", hldq[b], uq2[b], sgq, 1408,
                            uguard=(tV1, tTrU[b]))
                tHq = S.op("vector", lambda e, b=b: e.scalar_tensor_tensor(
                    uq2[b], hldq[b], -0.5, uq2[b], ALU.add, ALU.subtract),
                    inc="sV")
                tFreeLd[b] = tHq

                def mk_pend(tc=tc, q=q, b=b, tHq=tHq, qi=qidx):
                    def emitp():
                        tTr = None
                        for sub in range(3):
                            jmax = 4 if sub < 2 else 3
                            bq = 6 + ((qi * 3 + sub) % 2)
                            S.wait("tensor", tHq if sub == 0 else None,
                                   tEvB[bq])
                            for jj in range(jmax):
                                c = 4 * sub + jj
                                pb = PS[:, 512 * bq + 128 * jj:
                                        512 * bq + 128 * (jj + 1)]
                                tTr = S.op(
                                    "tensor", lambda e, pb=pb, b=b, c=c:
                                    e.transpose(
                                        pb, uq2[b][:, 128 * c:128 * (c + 1)],
                                        ident),
                                    inc="sT" if jj == jmax - 1 else None)
                            S.wait("scalar", tTr,
                                   tSpSt[b] if sub == 0 else None)
                            tEvB[bq] = S.op(
                                "scalar",
                                lambda e, b=b, sub=sub, bq=bq, jmax=jmax:
                                e.copy(hqstq[b][:, 512 * sub:
                                                512 * sub + 128 * jmax],
                                       bank(bq, 128 * jmax)), inc="sS")
                        tTrU[b] = tTr
                        S.wait("gpsimd", tEvB[6], tEvB[7])
                        dsth = shqT_h[1408 * q:1408 * (q + 1),
                                      128 * tc:128 * (tc + 1)].rearrange(
                            "(cc p) t -> p cc t", p=128)
                        tSpSt[b] = S.dma("gpsimd", dsth, hqstq[b],
                                         f"hqstq{b}")
                        spHq[(tc, q)] = tSpSt[b]
                    return emitp

                tPend[0] = mk_pend()
                qidx += 1
                yield (tc, q)
        if tPend[0] is not None:
            tPend[0]()
            tPend[0] = None

    p4b_iter = None
    if "p4b" in phases:
        p4b_iter = p4b_gen()
        for _ in range(16):
            next(p4b_iter)
        if "p5" not in phases:
            drain(p4b_iter)
            p4b_iter = None
            S.barrier()

    hqsc = [buf(f"hqsc{i}", [128, NI * 512], BF16, 0 + 11 * i, "p5m")
            for i in range(2)]
    dwq = [buf(f"dwq{i}", [128, NI * 2048], BF16, 22 + 44 * i, "p5m")
           for i in range(2)]
    x2part = buf("x2part", [128, 4 * 2048], F32, 110, "p5m")
    x1nld = buf("x1nld", [128, D], F32, 142, "p5m")
    outst = buf("outst", [128, D], F32, 150, "p5m")

    if "p5" in phases:
        tDw = [None, None]
        tHqLd = [None, None]
        tMM5 = {}
        tEv5 = {}
        tX1l = {}
        tSttX1 = {}
        tOutDma = None
        for sc in range(4):
            for qtr in range(4):
                bq = qtr % 2
                S.wait("sync", tMM5.get((sc, qtr - 2, 3, 3)),
                       tMM5.get((sc - 1, qtr + 2, 3, 3)),
                       *[spHq[(4 * sc + i, qtr)] for i in range(4)])
                hsrc = shqT_h[1408 * qtr:1408 * (qtr + 1),
                              512 * sc:512 * (sc + 1)].rearrange(
                    "(cc p) t -> p cc t", p=128)
                tHqLd[bq] = S.dma("sync", hqsc[bq], hsrc, f"hqsc{bq}")
                tDw[bq] = S.dma("sync", dwq[bq],
                                sdwT_h[1408 * qtr:1408 * (qtr + 1), :].rearrange(
                                    "(cc p) f -> p cc f", p=128), f"dwq{bq}")
                for ts in range(4):
                    tglob = 4 * sc + ts
                    if qtr == 3:
                        S.wait("sync", tSttX1.get((sc, ts - 1)) if ts > 0
                               else tSttX1.get((sc - 1, 3)))
                        tX1l[(sc, ts)] = S.dma(
                            "sync", x1nld,
                            sx1n_h[128 * tglob:128 * (tglob + 1), :], "x1nld")
                    S.wait("tensor", tHqLd[bq], tDw[bq],
                           *[tEv5.get((sc, qtr, ts - 1, f4))
                             for f4 in range(4)],
                           *([tEv5.get((sc, qtr - 1, 3, f4))
                              for f4 in range(4)] if ts == 0 else []),
                           *([tEv5.get((sc - 1, 3, 3, f4))
                              for f4 in range(4)]
                             if (ts == 0 and qtr == 0) else []))
                    for c2 in range(NI):
                        for f4 in range(4):
                            tMM5[(sc, qtr, ts, f4)] = S.op(
                                "tensor",
                                lambda e, bq=bq, ts=ts, c2=c2, f4=f4:
                                e.matmul(
                                    bank(f4),
                                    hqsc[bq][:, 512 * c2 + 128 * ts:
                                             512 * c2 + 128 * (ts + 1)],
                                    dwq[bq][:, 2048 * c2 + 512 * f4:
                                            2048 * c2 + 512 * (f4 + 1)],
                                    start=(c2 == 0), stop=(c2 == NI - 1)),
                                inc="sT" if c2 == NI - 1 else None)
                    for f4 in range(4):
                        xp4 = x2part[:, 2048 * ts + 512 * f4:
                                     2048 * ts + 512 * (f4 + 1)]
                        if qtr == 0:
                            S.wait("scalar", tMM5[(sc, qtr, ts, f4)])
                            tEv5[(sc, qtr, ts, f4)] = S.op(
                                "scalar", lambda e, xp4=xp4, f4=f4:
                                e.copy(xp4, bank(f4)), inc="sS")
                        else:
                            S.wait("vector", tMM5[(sc, qtr, ts, f4)],
                                   tEv5.get((sc, 0, ts, f4))
                                   if qtr == 1 else None)
                            tEv5[(sc, qtr, ts, f4)] = S.op(
                                "vector", lambda e, xp4=xp4, f4=f4:
                                e.tensor_add(xp4, xp4, bank(f4)), inc="sV")
                    if qtr == 3:
                        xpt = x2part[:, 2048 * ts:2048 * (ts + 1)]
                        S.wait("vector", tX1l[(sc, ts)], tOutDma,
                               inv2tok[tglob])
                        tStt = S.op(
                            "vector", lambda e, xpt=xpt, tglob=tglob:
                            e.scalar_tensor_tensor(
                                outst, xpt, invsx2[:, tglob:tglob + 1],
                                x1nld, ALU.mult, ALU.add), inc="sV")
                        tSttX1[(sc, ts)] = tStt

                        def ss(fn):
                            tok = S.op("scalar", fn, inc="sS")
                            S.wait("scalar", tok)
                            return tok
                        S.wait("scalar", tStt)
                        ss(lambda e, xpt=xpt: e.activation(
                            xpt, outst, ACT.Identity,
                            accum_out=lnc[:, 0:1]))
                        ss(lambda e: e.activation(
                            lnc[:, 2:3], lnc[:, 0:1], ACT.Identity,
                            scale=-1.0 / 2048.0))
                        ss(lambda e: e.activation(
                            outst, outst, ACT.Identity,
                            bias=lnc[:, 2:3]))
                        ss(lambda e, xpt=xpt: e.activation(
                            xpt, outst, ACT.Square, accum_out=lnc[:, 4:5]))
                        tSq5 = ss(lambda e: e.activation(
                            lnc[:, 5:6], lnc[:, 4:5], ACT.Sqrt,
                            scale=1.0 / 2048.0, bias=sm[:, 9:10]))
                        S.wait("vector", tSq5)
                        tRc5 = vsmall(lambda e: e.reciprocal(
                            lnc[:, 6:7], lnc[:, 5:6]))
                        S.wait("scalar", tRc5)
                        tFin = ss(lambda e: e.activation(
                            outst, outst, ACT.Identity, scale=lnc[:, 6:7]))
                        S.wait("gpsimd", tFin)
                        tOutDma = S.dma(
                            "gpsimd", out_h[128 * tglob:128 * (tglob + 1), :],
                            outst, "outsp")
                    if p4b_iter is not None:
                        next(p4b_iter, None)

    S.barrier()
    S.build()
    return nc, S, locals()




from concourse.bass_utils import run_bass_kernel_spmd

_PROG_CACHE = {}


def kernel(L0_attn_x, L0_attn_wq, L0_attn_wk, L0_attn_wv, L0_attn_wo,
           L0_ffn_gate_w, L0_ffn_down_w, L0_ffn_gate_norm_w,
           L0_ffn_down_norm_w):
    tabs = host_tables()
    if "nc" not in _PROG_CACHE:
        nc, _S, _env = build_program(dbg=(), phases=ALLP)
        _PROG_CACHE["nc"] = nc
    nc = _PROG_CACHE["nc"]
    base = {
        "wq": np.ascontiguousarray(L0_attn_wq),
        "wk": np.ascontiguousarray(L0_attn_wk),
        "wv": np.ascontiguousarray(L0_attn_wv),
        "wo": np.ascontiguousarray(L0_attn_wo),
        "gw": np.ascontiguousarray(L0_ffn_gate_w),
        "dwm": np.ascontiguousarray(L0_ffn_down_w),
        "g1": np.ascontiguousarray(L0_ffn_gate_norm_w).reshape(1, -1),
        "g2": np.ascontiguousarray(L0_ffn_down_norm_w).reshape(1, -1),
        "cc2": tabs["ccP"], "ss2": tabs["ssP"],
        "identc": tabs["ident"], "onesc": tabs["ones128"],
    }
    in_maps = []
    for b in range(8):
        m = dict(base)
        m["x"] = np.ascontiguousarray(L0_attn_x[b])
        in_maps.append(m)
    res = run_bass_kernel_spmd(nc, in_maps, core_ids=list(range(8)))
    return np.stack([r["out"] for r in res.results], axis=0)


# version 18
# speedup vs baseline: 1.0165x; 1.0165x over previous
import numpy as np
import concourse.bass as bass
import concourse.mybir as mybir

F32 = mybir.dt.float32
BF16 = mybir.dt.bfloat16
AX = mybir.AxisListType.X
ALU = mybir.AluOpType
ACT = mybir.ActivationFunctionType

T, D, H, I2, INTER = 2048, 2048, 16, 11264, 5632
NCH, NT, NI = 16, 16, 11
NI2 = 44
EPS = 1e-5
ISQD = 1.0 / float(np.sqrt(np.float64(2048.0)))
KB = 1024
SBUF_BASE = 16512
SBUF_END = 229344


ALLP = ("p0", "p1", "p1b", "gw", "p2", "p3", "p4", "dw", "p4b", "p5")


class Sched:
    ENGS = ("tensor", "vector", "scalar", "gpsimd", "sync")

    def __init__(self, nc):
        self.nc = nc
        self.q = {e: [] for e in self.ENGS}
        self.semnames = []
        self.count = {}

    def sem(self, name):
        self.semnames.append(name)
        self.count[name] = 0

    def wait(self, eng, *toks):
        best = {}
        for t in toks:
            if t is None:
                continue
            s, v = t
            if best.get(s, -1) < v:
                best[s] = v
        for s, v in best.items():
            self.q[eng].append(("wait", s, v))

    def op(self, eng, fn, inc=None, incby=1):
        tok = None
        if inc is not None:
            self.count[inc] += incby
            tok = (inc, self.count[inc])
        self.q[eng].append(("op", fn, inc, incby))
        return tok

    def dma(self, eng, out, in_, slot, accum=None):
        name = "d:" + slot
        if name not in self.count:
            self.sem(name)

        def fn(e, out=out, in_=in_, accum=accum):
            kw = {"accum_op": accum} if accum is not None else {}
            return e.dma_start(out=out, in_=in_, **kw)
        return self.op(eng, fn, inc=name, incby=16)

    def barrier(self):
        toks = [(s, self.count[s]) for s in self.semnames if self.count[s] > 0]
        for e in self.ENGS:
            self.wait(e, *toks)

    def build(self):
        nc = self.nc
        ctxs = [nc.semaphore(n) for n in self.semnames]
        sems = {}
        for n, cm in zip(self.semnames, ctxs):
            sems[n] = cm.__enter__()
        with nc.Block() as block:
            for eng in self.ENGS:
                items = self.q[eng]
                if not items:
                    continue

                def make_body(items=items):
                    def body(e):
                        for it in items:
                            if it[0] == "wait":
                                e.wait_ge(sems[it[1]], it[2])
                            else:
                                _, fn, inc, incby = it
                                ins = fn(e)
                                if inc is not None:
                                    ins.then_inc(sems[inc], incby)
                    return body

                getattr(block, eng)(make_body())
        return nc


def host_tables():
    import jax
    import jax.numpy as jnp
    with jax.default_device(jax.devices("cpu")[0]):
        inv_freq = 1.0 / 10000.0 ** (jnp.arange(0, D, 2, dtype=jnp.float32) / D)
        tt = jnp.arange(T, dtype=jnp.float32)
        ang = jnp.outer(tt, inv_freq)
        cos = np.asarray(jnp.cos(ang), dtype=np.float32)
        sin = np.asarray(jnp.sin(ang), dtype=np.float32)
    ccP = np.empty((D, T), dtype=np.float32)
    ssP = np.empty((D, T), dtype=np.float32)
    for h in range(H):
        pr = cos.T[64 * h:64 * (h + 1), :]
        si = sin.T[64 * h:64 * (h + 1), :]
        ccP[128 * h:128 * h + 64] = pr
        ccP[128 * h + 64:128 * h + 128] = pr
        ssP[128 * h:128 * h + 64] = -si
        ssP[128 * h + 64:128 * h + 128] = si
    return dict(
        ccP=ccP, ssP=ssP,
        ident=np.eye(128, dtype=np.float32),
        ones128=np.ones((128, 128), dtype=np.float32),
    )


def build_program(dbg=(), phases=None):
    if phases is None:
        phases = ALLP
    nc = bass.Bass("TRN2", target_bir_lowering=False,
                   detect_race_conditions=False)

    def dram(name, shape, dtype, kind=None):
        if kind is None:
            kind = "ExternalOutput" if name in dbg else "Internal"
        return nc.dram_tensor(name, list(shape), dtype, kind=kind).ap()

    x_h = dram("x", [T, D], F32, kind="ExternalInput")
    w_h = {k: dram(k, [D, D], F32, kind="ExternalInput")
           for k in ("wq", "wk", "wv", "wo")}
    gw_h = dram("gw", [I2, D], F32, kind="ExternalInput")
    dwm_h = dram("dwm", [D, INTER], F32, kind="ExternalInput")
    g1_h = dram("g1", [1, D], F32, kind="ExternalInput")
    g2_h = dram("g2", [1, INTER], F32, kind="ExternalInput")
    cc2_h = dram("cc2", [D, T], F32, kind="ExternalInput")
    ss2_h = dram("ss2", [D, T], F32, kind="ExternalInput")
    ident_h = dram("identc", [128, 128], F32, kind="ExternalInput")
    ones_h = dram("onesc", [128, 128], F32, kind="ExternalInput")
    out_h = dram("out", [T, D], F32, kind="ExternalOutput")

    sq_h = dram("sp_q", [D, T], F32)
    sk_h = dram("sp_k", [D, T], F32)
    svT_h = dram("sp_vT", [D, T], F32)
    sv_h = dram("sp_v", [T, D], F32)
    scth_h = dram("sp_cth", [D, T], BF16)
    sctl_h = dram("sp_ctl", [D, T], BF16)
    sx1n_h = dram("sp_x1n", [T, D], F32)
    sxqT_h = dram("sp_xqT", [D, T], BF16)
    sh_h = dram("sp_h", [T, INTER], F32)
    shqT_h = dram("sp_hqT", [INTER, T], BF16)
    sgwT_h = dram("sp_gwT", [D, I2], BF16)
    sdwT_h = dram("sp_dwT", [INTER, D], BF16)
    dbgsm_h = dram("dbg_sm", [128, 64], F32)
    dbgx1_h = dram("dbg_x1", [T, D], F32)

    S = Sched(nc)
    for s in ("sT", "sV", "sS"):
        S.sem(s)

    def buf(name, shape, dtype, off_kb, phase):
        off = SBUF_BASE + int(off_kb * KB)
        nbytes = int(np.prod(shape[1:])) * (4 if dtype == F32 else 2)
        assert off % 32 == 0 and off + nbytes <= SBUF_END, (name, off_kb, nbytes)
        return nc.alloc_sbuf_tensor_at(name, list(shape), dtype, offset=off).ap()

    G1bc = buf("c_g1bc", [128, D], F32, 196, "all")
    ident = buf("c_ident", [128, 128], F32, 204, "all")
    ones128 = buf("c_ones", [128, 128], F32, 204.5, "all")
    sm = buf("c_sm", [128, 64], F32, 205, "all")
    invsx1 = buf("c_invsx1", [128, 16], F32, 205.25, "all")
    invsx2 = buf("c_invsx2", [128, 16], F32, 205.3125, "all")
    lnc = buf("c_ln", [128, 32], F32, 205.375, "all")
    hss = buf("c_hss", [128, 192], F32, 205.5, "all")
    hmx = buf("c_hmx", [128, 176], F32, 206.25, "all")

    PS = nc.alloc_psum_tensor("PS", [128, 4096], F32).ap()

    def bank(i, w=512, o=0):
        return PS[:, 512 * i + o: 512 * i + o + w]

    t_cid = S.dma("sync", ident, ident_h, "c_id")
    t_con = S.dma("sync", ones128, ones_h, "c_on")
    S.op("vector", lambda e: e.memset(sm, 0.0))
    S.op("vector", lambda e: e.memset(lnc, 0.0))
    S.op("vector", lambda e: e.memset(invsx1, 0.0))
    S.op("vector", lambda e: e.memset(invsx2, 0.0))
    S.op("vector", lambda e: e.memset(sm[:, 4:5], -0.5))
    S.op("vector", lambda e: e.memset(sm[:, 5:6], 0.5))
    S.op("vector", lambda e: e.memset(sm[:, 9:10], EPS))
    S.op("vector", lambda e: e.memset(sm[:, 10:11], -128.0))
    S.op("vector", lambda e: e.memset(sm[:, 11:12], 128.5))
    S.op("vector", lambda e: e.memset(sm[:, 6:7], 0.0))
    S.op("vector", lambda e: e.memset(sm[:, 7:8], 0.0))
    S.op("vector", lambda e: e.memset(hmx, 0.0))
    t_minit = S.op("vector", lambda e: e.memset(hss, 0.0), inc="sV")

    scanb = buf("scanb", [128, 2048], F32, 188, "p01")
    scan_tiles = []
    if "gw" in phases or "dw" in phases or "p4" in phases or "p5" in phases:
        gw3 = gw_h.rearrange("(n p) c -> n p c", p=128)
        dw3 = dwm_h.rearrange("(n p) c -> n p c", p=128)
        for n in range(I2 // 128):
            scan_tiles.append(gw3[n])
        for n in range(NCH):
            scan_tiles.append(dw3[n, :, 0:2048])
            scan_tiles.append(dw3[n, :, 2048:4096])
            scan_tiles.append(dw3[n, :, 4096:5632])
    scan_i = [0]
    N_GW_TILES = I2 // 128

    def emit_scan(nt):
        for _ in range(nt):
            i = scan_i[0]
            if i >= len(scan_tiles):
                return
            if i == 0:
                S.wait("scalar", t_minit)
            scan_i[0] += 1
            src = scan_tiles[i]
            w = src.shape[-1]
            tok = S.dma("scalar", scanb[:, 0:w], src, "scanb")
            S.wait("scalar", tok)
            S.op("scalar", lambda e, w=w, i=i: e.activation(
                scanb[:, 0:w], scanb[:, 0:w], ACT.Abs,
                accum_out=hss[:, i:i + 1]), inc="sS")

    xTh = buf("xTh", [128, NCH * T], BF16, 0, "p01")
    xTl = buf("xTl", [128, NCH * T], BF16, 64, "p01")
    xld = [buf(f"xld{i}", [128, D], F32, 128 + 8 * i, "p0") for i in range(2)]

    if "p0" in phases:
        S.wait("tensor", t_cid, t_con)
        tT = [None, None]
        tEv = [None, None]
        tEs = [None, None]
        for tc in range(NT):
            b = tc % 2
            S.wait("sync", tT[b])
            tload = S.dma("sync", xld[b], x_h[128 * tc:128 * (tc + 1), :], f"xld{b}")
            S.wait("tensor", tload, tEv[b], tEs[b])
            for c in range(NCH):
                pb = PS[:, 2048 * b + 128 * c: 2048 * b + 128 * (c + 1)]
                tT[b] = S.op("tensor",
                             lambda e, pb=pb, b=b, c=c: e.transpose(
                                 pb, xld[b][:, 128 * c:128 * (c + 1)], ident),
                             inc="sT" if c == NCH - 1 else None)
            S.wait("scalar", tT[b])
            for k in range(4):
                for j in range(4):
                    c = 4 * k + j
                    src = bank(4 * b + k)[:, 128 * j:128 * (j + 1)]
                    dh = xTh[:, 2048 * c + 128 * tc: 2048 * c + 128 * (tc + 1)]
                    tEs[b] = S.op("scalar", lambda e, d=dh, s=src: e.copy(d, s),
                                  inc="sS" if (k == 3 and j == 3) else None)
            S.wait("vector", tT[b], tEs[b])
            for k in range(4):
                for j in range(4):
                    c = 4 * k + j
                    src = bank(4 * b + k)[:, 128 * j:128 * (j + 1)]
                    dh = xTh[:, 2048 * c + 128 * tc: 2048 * c + 128 * (tc + 1)]
                    dl = xTl[:, 2048 * c + 128 * tc: 2048 * c + 128 * (tc + 1)]
                    tEv[b] = S.op("vector",
                                  lambda e, dl=dl, s=src, dh=dh:
                                  e.tensor_sub(dl, s, dh),
                                  inc="sV" if (k == 3 and j == 3) else None)
            emit_scan(2)

    S.barrier()

    wld = [buf(f"wld{i}", [128, D], F32, 128 + 8 * i, "p1") for i in range(2)]
    wTh = [buf(f"wTh{i}", [128, D], BF16, 144 + 4 * i, "p1") for i in range(2)]
    wTl = [buf(f"wTl{i}", [128, D], BF16, 152 + 4 * i, "p1") for i in range(2)]
    ccb = buf("ccb", [128, T], F32, 160, "p1")
    ssb = buf("ssb", [128, T], F32, 168, "p1")
    qst = buf("qst", [128, T], F32, 176, "p1")
    m1b = buf("m1b", [128, 512], F32, 184, "p1")
    m2b = buf("m2b", [128, 512], F32, 186, "p1")

    if "p1" in phases:
        tTw = None
        tMM = None
        tRope = None
        tRopePrev = {0: None, 1: None}
        tSpill = None
        tTabs0 = tTabs1 = None
        wsched = []
        for wname, spill in (("wq", sq_h), ("wk", sk_h), ("wv", svT_h)):
            for fc in range(NCH):
                wsched.append((wname, spill, fc))
        wprep = {}

        def emit_wprep(i):
            nonlocal tTw
            if i >= len(wsched) or i in wprep:
                return
            wname, _sp, fc = wsched[i]
            b = i % 2
            rope = wname != "wv"
            wsrc = w_h[wname]
            S.wait("sync", tTw, *(wprep.get(i - 2, (None, None))))
            if rope:
                srcw = wsrc[128 * fc:128 * (fc + 1), :].rearrange(
                    "(p two) c -> two p c", two=2)
            else:
                srcw = wsrc[128 * fc:128 * (fc + 1), :]
            tld = S.dma("sync", wld[b], srcw, f"wld{b}")
            S.wait("tensor", tld, *(wprep.get(i - 2, (None, None))),
                   *(wprep.get(i - 1, (None, None))))
            tTwl = None
            for c in range(NCH):
                pb = PS[:, 2048 + 128 * c: 2048 + 128 * (c + 1)]
                tTwl = S.op("tensor",
                            lambda e, pb=pb, b=b, c=c: e.transpose(
                                pb, wld[b][:, 128 * c:128 * (c + 1)], ident),
                            inc="sT" if c == NCH - 1 else None)
            tTw = tTwl
            S.wait("scalar", tTwl)
            tEsW = None
            for k in range(4):
                tEsW = S.op("scalar",
                            lambda e, b=b, k=k: e.copy(
                                wTh[b][:, 512 * k:512 * (k + 1)], bank(4 + k)),
                            inc="sS" if k == 3 else None)
            S.wait("vector", tEsW)
            tEvW = None
            for k in range(4):
                tEvW = S.op("vector",
                            lambda e, b=b, k=k: e.tensor_sub(
                                wTl[b][:, 512 * k:512 * (k + 1)], bank(4 + k),
                                wTh[b][:, 512 * k:512 * (k + 1)]),
                            inc="sV" if k == 3 else None)
            wprep[i] = (tEvW, tEsW)

        emit_wprep(0)
        for i in range(len(wsched)):
            wname, spill, fc = wsched[i]
            b = i % 2
            rope = wname != "wv"
            if True:
                tEvW, tEsW = wprep[i]
                if rope:
                    S.wait("sync", tRope)
                    tTabs0 = S.dma("sync", ccb,
                                   cc2_h[128 * fc:128 * (fc + 1), :], "ccb")
                    tTabs1 = S.dma("sync", ssb,
                                   ss2_h[128 * fc:128 * (fc + 1), :], "ssb")
                tRopeP = {0: tRopePrev[0], 1: tRopePrev[1]}
                for pair in range(2):
                    if pair == 1:
                        emit_wprep(i + 1)
                    S.wait("tensor", tEvW, tEsW, tRopePrev[pair])
                    nmm = 0
                    for c in range(NCH):
                        for pi, (wt, xt) in enumerate(
                                ((wTh[b], xTh), (wTh[b], xTl), (wTl[b], xTh))):
                            for t4 in (2 * pair, 2 * pair + 1):
                                nmm += 1
                                first = (c == 0 and pi == 0)
                                last = (c == NCH - 1 and pi == 2)
                                tMM = S.op(
                                    "tensor",
                                    lambda e, wt=wt, xt=xt, c=c, t4=t4,
                                    first=first, last=last: e.matmul(
                                        bank(t4),
                                        wt[:, 128 * c:128 * (c + 1)],
                                        xt[:, 2048 * c + 512 * t4:
                                           2048 * c + 512 * (t4 + 1)],
                                        start=first, stop=last),
                                    inc="sT" if nmm == 96 else None)
                    if rope:
                        S.wait("vector", tMM, tSpill, tTabs0, tTabs1)
                        for t4 in (2 * pair, 2 * pair + 1):
                            p = bank(t4)
                            S.op("vector", lambda e, p=p, t4=t4: e.tensor_mul(
                                m1b, p, ccb[:, 512 * t4:512 * (t4 + 1)]))
                            S.op("vector", lambda e, p=p, t4=t4: e.tensor_mul(
                                m2b[0:64, :], p[64:128, :],
                                ssb[0:64, 512 * t4:512 * (t4 + 1)]))
                            S.op("vector", lambda e, p=p, t4=t4: e.tensor_mul(
                                m2b[64:128, :], p[0:64, :],
                                ssb[64:128, 512 * t4:512 * (t4 + 1)]))
                            tRopeP[pair] = S.op(
                                "vector", lambda e, t4=t4: e.tensor_add(
                                    qst[:, 512 * t4:512 * (t4 + 1)], m1b, m2b),
                                inc="sV" if t4 % 2 == 1 else None)
                    else:
                        S.wait("scalar", tMM, tSpill)
                        for t4 in (2 * pair, 2 * pair + 1):
                            tRopeP[pair] = S.op(
                                "scalar", lambda e, t4=t4: e.copy(
                                    qst[:, 512 * t4:512 * (t4 + 1)], bank(t4)),
                                inc="sS" if t4 % 2 == 1 else None)
                tRopePrev = tRopeP
                tRope = tRopeP[1]
                S.wait("gpsimd", tRopeP[0], tRopeP[1])
                tSpill = S.dma("gpsimd",
                               spill[128 * fc:128 * (fc + 1), :], qst, "qstsp")
                emit_scan(5)

    S.barrier()

    if "gw" in phases or "dw" in phases or "p4" in phases or "p5" in phases:
        emit_scan(len(scan_tiles))
        tlast = ("sS", S.count["sS"])
        S.wait("vector", tlast)
        S.op("vector", lambda e: e.reduce_sum(
            sm[:, 6:7], hss[:, 0:N_GW_TILES], AX))
        tlast = S.op("vector", lambda e: e.reduce_sum(
            sm[:, 7:8], hss[:, N_GW_TILES:N_GW_TILES + 48], AX), inc="sV")
        S.wait("tensor", tlast)
        S.op("tensor", lambda e: e.matmul(bank(0, 1), ones128, sm[:, 6:7],
                                          start=True, stop=True))
        tm = S.op("tensor", lambda e: e.matmul(bank(0, 1, o=4), ones128,
                                               sm[:, 7:8],
                                               start=True, stop=True), inc="sT")
        S.wait("vector", tm)
        S.op("vector", lambda e: e.tensor_scalar(
            sm[:, 0:1], bank(0, 1), 1.0 / (I2 * D), 1e-5, ALU.mult, ALU.max))
        tf1 = S.op("vector", lambda e: e.tensor_scalar(
            sm[:, 1:2], bank(0, 1, o=4), 1.0 / (D * INTER), 1e-5,
            ALU.mult, ALU.max), inc="sV")
        S.wait("vector", tf1)
        S.op("vector", lambda e: e.reciprocal(sm[:, 2:3], sm[:, 0:1]))
        S.op("vector", lambda e: e.reciprocal(sm[:, 3:4], sm[:, 1:2]))
        S.op("vector", lambda e: e.tensor_scalar_mul(
            sm[:, 0:1], sm[:, 0:1], 0.5 / 127.0))
        tsmf = S.op("vector", lambda e: e.tensor_scalar_mul(
            sm[:, 1:2], sm[:, 1:2], 0.5 / 127.0), inc="sV")
        if "dbg_sm" in dbg:
            S.wait("gpsimd", tsmf)
            S.dma("gpsimd", dbgsm_h, sm, "dbgsm")
        S.barrier()

    dbgiv_h = dram("dbg_iv", [128, 16], F32)
    if "dbg_iv" in dbg:
        pass

    tzld = [buf(f"tzld{i}", [128, INTER], F32, 0 + 22 * i, "gwdw")
            for i in range(2)]
    tzs1 = buf("tzs1", [128, INTER], F32, 44, "gwdw")
    tzs2 = buf("tzs2", [128, INTER], F32, 66, "gwdw")
    tztern = buf("tztern", [128, INTER], F32, 88, "gwdw")
    tzst = [buf(f"tzst{i}", [128, INTER], BF16, 110 + 11 * i, "gwdw")
            for i in range(2)]

    def ternarize_gen(src3, n_chunks, width, scol, dst_h):
        nw = width // 128
        ngr = (nw + 3) // 4
        tld = [None, None]
        tSg = None
        tTn = None
        tTr_l = None
        tEvc = [None, None]
        tSp = [None, None]
        for n in range(n_chunks):
            b = n % 2
            S.wait("sync", tSg if n >= 2 else None)
            tld[b] = S.dma("sync", tzld[b][:, 0:width], src3[n], f"tzld{b}")
            S.wait("scalar", tld[b], tTn)
            S.op("scalar", lambda e, b=b, w=width, scol=scol: e.activation(
                tzs1[:, 0:w], tzld[b][:, 0:w], ACT.Sign,
                bias=sm[:, 4:5], scale=sm[:, scol:scol + 1]))
            tSg = S.op("scalar", lambda e, b=b, w=width, scol=scol: e.activation(
                tzs2[:, 0:w], tzld[b][:, 0:w], ACT.Sign,
                bias=sm[:, 5:6], scale=sm[:, scol:scol + 1]), inc="sS")
            S.wait("vector", tSg, tTr_l)
            tTn = S.op("vector", lambda e, w=width: e.tensor_add(
                tztern[:, 0:w], tzs1[:, 0:w], tzs2[:, 0:w]), inc="sV")
            for g in range(ngr):
                s = (n * ngr + g) % 2
                jmax = min(4, nw - 4 * g)
                S.wait("tensor", tTn, tEvc[s])
                for j in range(jmax):
                    c = 4 * g + j
                    pb = PS[:, 512 * (4 + s) + 128 * j:
                            512 * (4 + s) + 128 * (j + 1)]
                    tTr_l = S.op("tensor",
                                 lambda e, pb=pb, c=c: e.transpose(
                                     pb, tztern[:, 128 * c:128 * (c + 1)], ident),
                                 inc="sT" if j == jmax - 1 else None)
                S.wait("scalar", tTr_l, tSp[b] if g == 0 else None)
                tEvc[s] = S.op("scalar",
                               lambda e, s=s, g=g, b=b, jmax=jmax: e.copy(
                                   tzst[b][:, 512 * g:512 * g + 128 * jmax],
                                   bank(4 + s, 128 * jmax)),
                               inc="sS")
            S.wait("gpsimd", tEvc[0], tEvc[1])
            dst = dst_h[:, 128 * n:128 * (n + 1)].rearrange(
                "(cc p) i -> p cc i", p=128)
            tSp[b] = S.dma("gpsimd", dst, tzst[b][:, 0:nw * 128], f"tzst{b}")
            yield n

    def drain(gen):
        if gen is None:
            return
        for _ in gen:
            pass

    gw_gen = None
    if "gw" in phases:
        gw3b = gw_h.rearrange("(n p) c -> n p c", p=128)
        gw_gen = ternarize_gen(gw3b, I2 // 128, D, 2, sgwT_h)

    vld = [buf(f"vld{i}", [128, D], F32, 128 + 8 * i, "p1b") for i in range(2)]
    vstg = [buf(f"vstg{i}", [128, 512], F32, 144 + 2 * i, "p1b") for i in range(2)]

    if "p1b" in phases:
        tTv = None
        tTvPrev = [None, None]
        tEvac = [None, None]
        tSpl = [None, None]
        for fc in range(NCH):
            b = fc % 2
            S.wait("sync", tTvPrev[b])
            tld = S.dma("sync", vld[b], svT_h[128 * fc:128 * (fc + 1), :], f"vld{b}")
            S.wait("tensor", tld)
            for g in range(4):
                s = (fc * 4 + g) % 2
                S.wait("tensor", tEvac[s])
                tTv = None
                for j in range(4):
                    ts_ = 4 * g + j
                    pb = PS[:, 2048 * s + 128 * j: 2048 * s + 128 * (j + 1)]
                    tTv = S.op("tensor",
                               lambda e, pb=pb, b=b, ts_=ts_: e.transpose(
                                   pb, vld[b][:, 128 * ts_:128 * (ts_ + 1)], ident),
                               inc="sT" if j == 3 else None)
                if g == 3:
                    tTvPrev[b] = tTv
                S.wait("scalar", tTv, tSpl[s])
                tEvac[s] = S.op("scalar",
                                lambda e, s=s: e.copy(vstg[s], bank(4 * s)),
                                inc="sS")
                S.wait("gpsimd", tEvac[s])
                dst = sv_h[512 * g:512 * (g + 1),
                           128 * fc:128 * (fc + 1)].rearrange(
                    "(j p) f -> p j f", p=128)
                tSpl[s] = S.dma("gpsimd", dst, vstg[s], f"vstg{s}")
            emit_scan(2)
            if gw_gen is not None:
                for _ in range(6):
                    next(gw_gen, None)

    drain(gw_gen)
    gw_gen = None
    S.barrier()

    qTb = [buf(f"qTb{i}", [128, T], F32, 0 + 8 * i, "p2") for i in range(2)]
    kTb = [buf(f"kTb{i}", [128, T], F32, 16 + 8 * i, "p2") for i in range(2)]
    vb = [buf(f"vb{i}", [128, T], F32, 32 + 8 * i, "p2") for i in range(2)]
    ebuf = buf("ebuf", [128, 16 * 1024], F32, 48, "p2")
    Eacc = buf("Eacc", [128, 1024], F32, 112, "p2")
    rbc = buf("rbc", [128, 1024], F32, 116, "p2")
    mctx = buf("mctx", [128, 1024], F32, 120, "p2")
    ctst = [buf(f"ctst{i}", [128, 2 * T], BF16, 124 + 8 * i, "p2")
            for i in range(2)]

    if "p2" in phases:
        tld_q = [None, None]
        tSc = None
        tExp = {}
        tEadd = None
        tFold = None
        tRec = None
        tCtx = None
        tCevK = None
        tCevS = None
        tSpC = [None, None]
        tQfree = [None, None]
        for h in range(H):
            b = h % 2
            S.wait("sync", tQfree[b])
            tq0 = S.dma("sync", qTb[b], sq_h[128 * h:128 * (h + 1), :], f"qTb{b}")
            tq1 = S.dma("sync", kTb[b], sk_h[128 * h:128 * (h + 1), :], f"kTb{b}")
            vsrc = sv_h[:, 128 * h:128 * (h + 1)].rearrange(
                "(kc p) f -> p kc f", p=128)
            tq2 = S.dma("sync", vb[b], vsrc, f"vb{b}")
            tld_q[b] = (tq0, tq1, tq2)
            for qbp in range(2):
                for kc in range(NCH):
                    pr = kc % 2
                    tprev = tExp.get(kc - 2) if kc >= 2 else tExp.get(14 + kc)
                    S.wait("tensor", *(tld_q[b] or (None,)), tprev, tFold)
                    for qh in range(2):
                        tSc = S.op(
                            "tensor",
                            lambda e, b=b, kc=kc, qh=qh, qbp=qbp, pr=pr: e.matmul(
                                bank(2 * pr + qh),
                                kTb[b][:, 128 * kc:128 * (kc + 1)],
                                qTb[b][:, 1024 * qbp + 512 * qh:
                                       1024 * qbp + 512 * (qh + 1)],
                                start=True, stop=True),
                            inc="sT" if qh == 1 else None)
                    S.wait("scalar", tSc)
                    for qh in range(2):
                        tExp[kc] = S.op(
                            "scalar",
                            lambda e, kc=kc, qh=qh, pr=pr: e.activation(
                                ebuf[:, 1024 * kc + 512 * qh:
                                     1024 * kc + 512 * (qh + 1)],
                                bank(2 * pr + qh), ACT.Exp, scale=ISQD),
                            inc="sS" if qh == 1 else None)
                    S.wait("vector", tExp[kc])
                    if kc == 0:
                        tEadd = S.op("vector", lambda e, kc=kc: e.tensor_copy(
                            Eacc, ebuf[:, 1024 * kc:1024 * (kc + 1)]),
                            inc="sV")
                    else:
                        tEadd = S.op("vector", lambda e, kc=kc: e.tensor_add(
                            Eacc, Eacc, ebuf[:, 1024 * kc:1024 * (kc + 1)]),
                            inc="sV" if kc == NCH - 1 else None)
                S.wait("tensor", tCevK, tCevS)
                for kc in range(NCH):
                    for qh in range(2):
                        tCtx = S.op(
                            "tensor",
                            lambda e, b=b, kc=kc, qh=qh, qbp=qbp: e.matmul(
                                bank(4 + qh),
                                vb[b][:, 128 * kc:128 * (kc + 1)],
                                ebuf[:, 1024 * kc + 512 * qh:
                                     1024 * kc + 512 * (qh + 1)],
                                start=(kc == 0), stop=(kc == NCH - 1)),
                            inc="sT" if (kc == NCH - 1 and qh == 1) else None)
                if qbp == 1:
                    tQfree[b] = tCtx
                S.wait("tensor", tEadd, tRec)
                for qh in range(2):
                    tFold = S.op("tensor", lambda e, qh=qh: e.matmul(
                        bank(6 + qh), ones128, Eacc[:, 512 * qh:512 * (qh + 1)],
                        start=True, stop=True),
                        inc="sT" if qh == 1 else None)
                S.wait("vector", tFold)
                tRec = S.op("vector", lambda e: e.reciprocal(
                    rbc, PS[:, 3072:4096]), inc="sV")
                S.wait("vector", tCtx, tRec, tSpC[b])
                tCevK = S.op("vector", lambda e: e.tensor_mul(
                    mctx, PS[:, 2048:3072], rbc), inc="sV")
                S.wait("scalar", tCevK)
                tCevS = S.op("scalar", lambda e, b=b, qbp=qbp: e.copy(
                    ctst[b][:, 1024 * qbp:1024 * (qbp + 1)], mctx), inc="sS")
                S.wait("vector", tCevS)
                tCevK = S.op("vector", lambda e, b=b, qbp=qbp: e.tensor_sub(
                    ctst[b][:, 2048 + 1024 * qbp:2048 + 1024 * (qbp + 1)],
                    mctx, ctst[b][:, 1024 * qbp:1024 * (qbp + 1)]), inc="sV")
            S.wait("gpsimd", tCevK, tCevS)
            S.dma("gpsimd", scth_h[128 * h:128 * (h + 1), :],
                  ctst[b][:, 0:2048], f"ctst{b}")
            tSpC[b] = S.dma("gpsimd", sctl_h[128 * h:128 * (h + 1), :],
                            ctst[b][:, 2048:4096], f"ctst{b}")

    S.barrier()

    woTh = buf("woTh", [128, NCH * T], BF16, 0, "p3")
    woTl = buf("woTl", [128, NCH * T], BF16, 64, "p3")
    wld2 = [buf(f"wld2{i}", [128, D], F32, 128 + 8 * i, "p3prep") for i in range(2)]
    ctile = buf("ctile", [128, D], F32, 128, "p3")
    xqb = buf("xqb", [128, D], F32, 136, "p3")
    xqstage = buf("xqstage", [128, D], BF16, 144, "p3")
    cthc = [buf(f"cthc{i}", [128, D], BF16, 148 + 4 * i, "p3") for i in range(2)]
    ctlc = [buf(f"ctlc{i}", [128, D], BF16, 156 + 4 * i, "p3") for i in range(2)]
    xch = [buf(f"xch{i}", [128, D], F32, 164 + 8 * i, "p3") for i in range(2)]
    xAt = buf("xAt", [128, D], F32, 180, "p3")
    xBt = buf("xBt", [128, D], F32, 188, "p3")

    def vsmall(fn):
        tok = S.op("vector", fn, inc="sV")
        S.wait("vector", tok)
        return tok

    def emit_ladder(eng, vtile, utile, cbt, w, uguard=None):
        w2 = w // 2
        halves = [(slice(0, w2),), (slice(w2, w),)]
        if uguard is None:
            uguard = ()
        elif not isinstance(uguard, tuple):
            uguard = (uguard,)
        tS = [None, None]
        tV = [list(uguard), list(uguard)]
        for bi, bv in enumerate((128.0, 64.0, 32.0, 16.0, 8.0, 4.0, 2.0, 1.0)):
            for hi, (sl,) in enumerate(halves):
                src_u = vtile if bi == 0 else utile
                S.wait("scalar", *tV[hi])
                tS[hi] = S.op("scalar", lambda e, s=src_u, sl=sl: e.activation(
                    cbt[:, sl], s[:, sl], ACT.Sign, bias=sm[:, 10:11]),
                    inc="sS")
                S.wait("vector", tS[hi])
                tV[hi] = [S.op("vector",
                               lambda e, s=src_u, sl=sl, bv=bv:
                               e.scalar_tensor_tensor(
                                   utile[:, sl], cbt[:, sl], -bv / 2.0,
                                   s[:, sl], ALU.mult, ALU.add), inc="sV")]

    if "p3" in phases:
        tTw = [None, None]
        tEv2 = [None, None]
        tEs2 = [None, None]
        for r in range(NCH):
            b = r % 2
            S.wait("sync", tTw[b])
            tld = S.dma("sync", wld2[b], w_h["wo"][128 * r:128 * (r + 1), :], f"wld2{b}")
            S.wait("tensor", tld, tEv2[b], tEs2[b])
            for c in range(NCH):
                pb = PS[:, 2048 * b + 128 * c: 2048 * b + 128 * (c + 1)]
                tTw[b] = S.op("tensor",
                              lambda e, pb=pb, b=b, c=c: e.transpose(
                                  pb, wld2[b][:, 128 * c:128 * (c + 1)], ident),
                              inc="sT" if c == NCH - 1 else None)
            S.wait("scalar", tTw[b])
            for k in range(4):
                for j in range(4):
                    c = 4 * k + j
                    srcp = bank(4 * b + k)[:, 128 * j:128 * (j + 1)]
                    dh = woTh[:, 2048 * c + 128 * r: 2048 * c + 128 * (r + 1)]
                    tEs2[b] = S.op("scalar", lambda e, d=dh, s=srcp: e.copy(d, s),
                                   inc="sS" if (k == 3 and j == 3) else None)
            S.wait("vector", tTw[b], tEs2[b])
            for k in range(4):
                for j in range(4):
                    c = 4 * k + j
                    srcp = bank(4 * b + k)[:, 128 * j:128 * (j + 1)]
                    dh = woTh[:, 2048 * c + 128 * r: 2048 * c + 128 * (r + 1)]
                    dl = woTl[:, 2048 * c + 128 * r: 2048 * c + 128 * (r + 1)]
                    tEv2[b] = S.op("vector",
                                   lambda e, dl=dl, s=srcp, dh=dh:
                                   e.tensor_sub(dl, s, dh),
                                   inc="sV" if (k == 3 and j == 3) else None)
        tg = S.dma("sync", G1bc[0:1, :], g1_h, "g1")
        S.wait("tensor", tg, tEv2[0], tEv2[1], tEs2[0], tEs2[1])
        tgb = None
        for j in range(4):
            tgb = S.op("tensor", lambda e, j=j: e.matmul(
                bank(j), ones128[0:1, :], G1bc[0:1, 512 * j:512 * (j + 1)],
                start=True, stop=True), inc="sT" if j == 3 else None)
        S.wait("scalar", tgb)
        tgs = None
        for j in range(4):
            tgs = S.op("scalar", lambda e, j=j: e.copy(
                G1bc[:, 512 * j:512 * (j + 1)], bank(j)),
                inc="sS" if j == 3 else None)

        tMMo = None
        tLd = [None, None]
        tFree = [None, None]
        tSpXQ = None
        tSpX1N = None
        tXqT = None
        tEvXq = tEv2[1]
        tAddPrev = {0: tgs, 1: tgs}
        tXqPend = [None]
        for tc in range(NT):
            b = tc % 2
            S.wait("sync", *(tFree[b] or (None,)))
            for m in range(4):
                csrc = scth_h[512 * m:512 * (m + 1),
                              128 * tc:128 * (tc + 1)].rearrange(
                    "(fc p) t -> p fc t", p=128)
                tc0 = S.dma("sync", cthc[b][:, 512 * m:512 * (m + 1)], csrc, f"cthc{b}")
            for m in range(4):
                csrc = sctl_h[512 * m:512 * (m + 1),
                              128 * tc:128 * (tc + 1)].rearrange(
                    "(fc p) t -> p fc t", p=128)
                tc1 = S.dma("sync", ctlc[b][:, 512 * m:512 * (m + 1)], csrc, f"ctlc{b}")
            tc2 = S.dma("sync", xch[b], x_h[128 * tc:128 * (tc + 1), :], f"xch{b}")
            tLd[b] = (tc0, tc1, tc2)
            tMMp = {}
            for pair in range(2):
                S.wait("tensor", *(tLd[b] or (None,)), tAddPrev[pair], tgs)
                nmm = 0
                for fc in range(NCH):
                    for pi, (ct, wt) in enumerate(
                            ((cthc[b], woTh), (cthc[b], woTl), (ctlc[b], woTh))):
                        for f4 in (2 * pair, 2 * pair + 1):
                            nmm += 1
                            tMMp[pair] = S.op(
                                "tensor",
                                lambda e, ct=ct, wt=wt, fc=fc, f4=f4,
                                first=(fc == 0 and pi == 0),
                                last=(fc == NCH - 1 and pi == 2): e.matmul(
                                    bank(f4),
                                    ct[:, 128 * fc:128 * (fc + 1)],
                                    wt[:, 2048 * fc + 512 * f4:
                                       2048 * fc + 512 * (f4 + 1)],
                                    start=first, stop=last),
                                inc="sT" if nmm == 96 else None)
            tMMo = tMMp[1]
            if tXqPend[0] is not None:
                tXqPend[0]()
                tXqPend[0] = None
            S.wait("vector", tMMp[0], tSpX1N)
            tadd0 = S.op("vector", lambda e, b=b: e.tensor_add(
                xAt[:, 0:1024], PS[:, 0:1024], xch[b][:, 0:1024]), inc="sV")
            S.wait("vector", tMMp[1])
            tadd = S.op("vector", lambda e, b=b: e.tensor_add(
                xAt[:, 1024:2048], PS[:, 1024:2048], xch[b][:, 1024:2048]),
                inc="sV")
            tAddPrev = {0: tadd0, 1: tadd}
            tFree[b] = (tMMo, tadd)
            if "dbg_x1" in dbg:
                S.wait("gpsimd", tadd)
                tdbg = S.dma("gpsimd", dbgx1_h[128 * tc:128 * (tc + 1), :],
                             xAt, "dbgx1")
                S.wait("vector", tdbg)
            vsmall(lambda e: e.reduce_sum(lnc[:, 0:1], xAt, AX))
            vsmall(lambda e: e.tensor_scalar_mul(
                lnc[:, 2:3], lnc[:, 0:1], -1.0 / 2048.0))
            S.op("vector", lambda e: e.tensor_scalar_add(
                xBt, xAt, lnc[:, 2:3]))
            S.op("vector", lambda e: e.tensor_mul(ctile, xBt, xBt))
            tV = S.op("vector", lambda e: e.reduce_sum(
                lnc[:, 4:5], ctile, AX), inc="sV")
            S.wait("scalar", tV)
            tSq2 = S.op("scalar", lambda e: e.activation(
                lnc[:, 5:6], lnc[:, 4:5], ACT.Sqrt, scale=1.0 / 2048.0,
                bias=sm[:, 9:10]), inc="sS")
            S.wait("vector", tSq2)
            vsmall(lambda e: e.reciprocal(lnc[:, 6:7], lnc[:, 5:6]))
            tX1n = S.op("vector", lambda e: e.tensor_scalar_mul(
                xBt, xBt, lnc[:, 6:7]), inc="sV")
            S.wait("gpsimd", tX1n)
            tSpX1N = S.dma("gpsimd", sx1n_h[128 * tc:128 * (tc + 1), :], xBt, "x1nsp")
            S.op("vector", lambda e: e.tensor_mul(ctile, xBt, xBt))
            tSq3 = S.op("vector", lambda e: e.reduce_sum(
                lnc[:, 7:8], ctile, AX), inc="sV")
            S.wait("scalar", tSq3)
            tRs = S.op("scalar", lambda e: e.activation(
                lnc[:, 8:9], lnc[:, 7:8], ACT.Sqrt, scale=1.0 / 2048.0,
                bias=sm[:, 9:10]), inc="sS")
            S.wait("vector", tRs)
            vsmall(lambda e: e.reciprocal(lnc[:, 9:10], lnc[:, 8:9]))
            tXn = S.op("vector", lambda e: e.scalar_tensor_tensor(
                xAt, xBt, lnc[:, 9:10], G1bc, ALU.mult, ALU.mult), inc="sV")
            vsmall(lambda e: e.reduce_max(
                lnc[:, 10:11], xAt, AX, apply_absolute_value=True))
            tAM = vsmall(lambda e: e.tensor_single_scalar(
                lnc[:, 10:11], lnc[:, 10:11], 1e-5, ALU.max))
            vsmall(lambda e: e.reciprocal(lnc[:, 11:12], lnc[:, 10:11]))
            tIS2 = vsmall(lambda e: e.tensor_scalar_mul(
                lnc[:, 11:12], lnc[:, 11:12], 127.0))
            S.wait("scalar", tAM)
            S.op("scalar", lambda e, tc=tc: e.activation(
                invsx1[:, tc:tc + 1], lnc[:, 10:11], ACT.Identity,
                scale=sm[:, 0:1]))
            tVq = S.op("vector", lambda e: e.tensor_scalar(
                xAt, xAt, lnc[:, 11:12], 128.5, ALU.mult, ALU.add), inc="sV")
            emit_ladder("vector", xAt, xBt, ctile, D, uguard=(tSpX1N, tVq))
            S.wait("vector", tXqT)
            tXq = S.op("vector", lambda e: e.scalar_tensor_tensor(
                xqb, xAt, -0.5, xBt, ALU.add, ALU.subtract), inc="sV")
            def emit_xq(tc=tc, tXq=tXq):
                nonlocal tXqT, tEvXq, tSpXQ
                S.wait("tensor", tXq, tEvXq, tEs2[1])
                for c in range(NCH):
                    pb = PS[:, 2048 + 128 * c:2048 + 128 * (c + 1)]
                    tXqT = S.op("tensor", lambda e, pb=pb, c=c: e.transpose(
                        pb, xqb[:, 128 * c:128 * (c + 1)], ident),
                        inc="sT" if c == NCH - 1 else None)
                S.wait("scalar", tXqT, tSpXQ)
                for k in range(4):
                    tEvXq = S.op("scalar", lambda e, k=k: e.copy(
                        xqstage[:, 512 * k:512 * (k + 1)], bank(4 + k)),
                        inc="sS" if k == 3 else None)
                S.wait("gpsimd", tEvXq)
                dstq = sxqT_h[:, 128 * tc:128 * (tc + 1)].rearrange(
                    "(cc p) t -> p cc t", p=128)
                tSpXQ = S.dma("gpsimd", dstq, xqstage, "xqsp")

            tXqPend[0] = emit_xq
        if tXqPend[0] is not None:
            tXqPend[0]()
            tXqPend[0] = None

    if "dbg_iv" in dbg and "p3" in phases:
        tiv = ("sS", S.count["sS"])
        S.wait("gpsimd", tiv, ("sV", S.count["sV"]))
        S.dma("gpsimd", dbgiv_h, invsx1, "dbgiv")
    S.barrier()

    xqT = buf("xqT", [128, NCH * T], BF16, 0, "p4")
    gwg = [buf(f"gwg{i}", [128, NCH * 512], BF16, 64 + 16 * i, "p4")
           for i in range(2)]
    gwu = [buf(f"gwu{i}", [128, NCH * 512], BF16, 96 + 16 * i, "p4")
           for i in range(2)]
    sgb = [buf(f"sgb{i}", [128, 512], F32, 128 + 2 * i, "p4") for i in range(2)]
    hstg = [buf(f"hstg{i}", [128, 512], F32, 132 + 2 * i, "p4") for i in range(2)]
    sqd = buf("sqd", [128, 512], F32, 136, "p4")
    G2bc4 = buf("G2bc4", [128, INTER], F32, 138, "p4")

    tzq = [buf(f"tzq{i}", [128, 1408], F32, 160 + 5.5 * i, "p4t")
           for i in range(2)]
    tzs1q = [buf(f"tzs1q{i}", [128, 1408], F32, 171 + 5.5 * i, "p4t")
             for i in range(2)]
    tzs2q = buf("tzs2q", [128, 1408], F32, 182, "p4t")
    tzstq = [buf(f"tzstq{i}", [128, 512], BF16, 187.5 + 1 * i, "p4t")
             for i in range(2)]

    def tern_dw_gen(tEvB2init):
        dw3b = dwm_h.rearrange("(n p) c -> n p c", p=128)
        tEvB2 = dict(tEvB2init)
        tSp2 = [None, None]
        tLd2 = [None, None]
        tSgLast = [None, None]
        tTrT = [None, None]
        tAddT = None
        pend2 = []
        step = 0
        for n in range(NCH):
            for q in range(4):
                sq = step % 2
                if len(pend2) >= 1:
                    pend2.pop(0)()
                S.wait("sync", tSgLast[sq])
                tLd2[sq] = S.dma("sync", tzq[sq],
                                 dw3b[n][:, 1408 * q:1408 * (q + 1)],
                                 f"tzq{sq}")
                S.wait("scalar", tLd2[sq], tTrT[sq], tAddT)
                S.op("scalar", lambda e, sq=sq: e.activation(
                    tzs1q[sq], tzq[sq], ACT.Sign, bias=sm[:, 4:5],
                    scale=sm[:, 3:4]))
                tSg = S.op("scalar", lambda e, sq=sq: e.activation(
                    tzs2q, tzq[sq], ACT.Sign, bias=sm[:, 5:6],
                    scale=sm[:, 3:4]), inc="sS")
                tSgLast[sq] = tSg
                S.wait("vector", tSg)
                tAdd = S.op("vector", lambda e, sq=sq: e.tensor_add(
                    tzs1q[sq], tzs1q[sq], tzs2q), inc="sV")
                tAddT = tAdd

                def mk_tpend(n=n, q=q, sq=sq, tAdd=tAdd, st=step):
                    def emitp():
                        tTr = None
                        for sub in range(3):
                            jmax = 4 if sub < 2 else 3
                            bk = 2 + ((st * 3 + sub) % 2)
                            hb = (st * 3 + sub) % 2
                            S.wait("tensor", tAdd if sub == 0 else None,
                                   tEvB2.get(bk))
                            for jj in range(jmax):
                                c = 4 * sub + jj
                                pb = PS[:, 512 * bk + 128 * jj:
                                        512 * bk + 128 * (jj + 1)]
                                tTr = S.op(
                                    "tensor", lambda e, pb=pb, sq=sq, c=c:
                                    e.transpose(
                                        pb,
                                        tzs1q[sq][:, 128 * c:128 * (c + 1)],
                                        ident),
                                    inc="sT" if jj == jmax - 1 else None)
                            S.wait("scalar", tTr, tSp2[hb])
                            tEvB2[bk] = S.op(
                                "scalar",
                                lambda e, hb=hb, bk=bk, jmax=jmax:
                                e.copy(tzstq[hb][:, 0:128 * jmax],
                                       bank(bk, 128 * jmax)), inc="sS")
                            S.wait("scalar", tEvB2[bk])
                            nr = 128 * jmax
                            r0 = 1408 * q + 512 * sub
                            dst = sdwT_h[r0:r0 + nr,
                                         128 * n:128 * (n + 1)].rearrange(
                                "(cc p) i -> p cc i", p=128)
                            tSp2[hb] = S.dma("scalar", dst,
                                             tzstq[hb][:, 0:nr],
                                             f"tzspl{hb}")
                        tTrT[sq] = tTr
                    return emitp

                pend2.append(mk_tpend())
                step += 1
                yield step
        while pend2:
            pend2.pop(0)()

    if "p4" in phases:
        tg2 = S.dma("sync", G2bc4[0:1, :], g2_h, "g2")
        S.wait("tensor", tg2)
        tEvJ = {}
        for jj in range(NI):
            bb = 2 + jj % 2
            S.wait("tensor", tEvJ.get(jj - 2))
            tb2 = S.op("tensor", lambda e, jj=jj, bb=bb: e.matmul(
                bank(bb), ones128[0:1, :], G2bc4[0:1, 512 * jj:512 * (jj + 1)],
                start=True, stop=True), inc="sT")
            S.wait("scalar", tb2)
            tEvJ[jj] = S.op("scalar", lambda e, jj=jj, bb=bb: e.copy(
                G2bc4[:, 512 * jj:512 * (jj + 1)], bank(bb)), inc="sS")
        tG2rdy = tEvJ[NI - 1]
        tern_it = (tern_dw_gen({2: tEvJ[NI - 1], 3: tEvJ[NI - 2]})
                   if "dw" in phases else None)
        tern_cnt = [0]
        tXQ = None
        for c in range(NCH):
            tXQ = S.dma("sync", xqT[:, 2048 * c:2048 * (c + 1)],
                        sxqT_h[128 * c:128 * (c + 1), :], "xqTld")
        tGld = [None, None]
        tMg = {}
        tEvG = {}
        tSpH = [None, None]
        for j in range(NI):
            b = j % 2
            S.wait("sync", tMg.get((j - 2, NT - 1)))
            for m in range(4):
                gsrc = sgwT_h[512 * m:512 * (m + 1),
                              512 * j:512 * (j + 1)].rearrange(
                    "(cc p) i -> p cc i", p=128)
                tGg = S.dma("sync", gwg[b][:, 2048 * m:2048 * (m + 1)], gsrc, f"gwg{b}")
            for m in range(4):
                usrc = sgwT_h[512 * m:512 * (m + 1),
                              INTER + 512 * j:INTER + 512 * (j + 1)].rearrange(
                    "(cc p) i -> p cc i", p=128)
                tGu = S.dma("sync", gwu[b][:, 2048 * m:2048 * (m + 1)],
                                usrc, f"gwu{b}")
            tGld[b] = (tGg, tGu)
            for tc in range(NT):
                s = tc % 2
                gb, ub = 4 * s, 4 * s + 1
                S.wait("tensor", tXQ, *(tGld[b] or (None,)), tEvG.get((j, tc - 2)),
                       tEvG.get((j - 1, tc + 14)))
                for c in range(NCH):
                    off = 2048 * (c // 4) + 512 * (c % 4)
                    S.op("tensor", lambda e, c=c, tc=tc, b=b, gb=gb, off=off:
                         e.matmul(bank(gb),
                                  xqT[:, 2048 * c + 128 * tc:
                                      2048 * c + 128 * (tc + 1)],
                                  gwg[b][:, off:off + 512],
                                  start=(c == 0), stop=(c == NCH - 1)))
                    tMg[(j, tc)] = S.op(
                        "tensor", lambda e, c=c, tc=tc, b=b, ub=ub, off=off:
                        e.matmul(bank(ub),
                                 xqT[:, 2048 * c + 128 * tc:
                                     2048 * c + 128 * (tc + 1)],
                                 gwu[b][:, off:off + 512],
                                 start=(c == 0), stop=(c == NCH - 1)),
                        inc="sT" if c == NCH - 1 else None)
                S.wait("scalar", tMg[(j, tc)],
                       tEvG.get((j, tc - 2)), tEvG.get((j - 1, tc + 14)))
                tSg2 = S.op("scalar", lambda e, s=s, gb=gb, tc=tc: e.activation(
                    sgb[s], bank(gb), ACT.Silu, scale=invsx1[:, tc:tc + 1]),
                    inc="sS")
                S.wait("vector", tSg2, tSpH[s],
                       tG2rdy if (j == 0 and tc == 0) else None)
                tH = S.op("vector", lambda e, s=s, ub=ub, tc=tc:
                          e.scalar_tensor_tensor(
                              hstg[s], bank(ub), invsx1[:, tc:tc + 1], sgb[s],
                              ALU.mult, ALU.mult), inc="sV")
                S.op("vector", lambda e, s=s: e.tensor_mul(
                    sqd, hstg[s], hstg[s]))
                S.op("vector", lambda e, tc=tc, j=j: e.reduce_sum(
                    hss[:, 11 * tc + j:11 * tc + j + 1], sqd, AX))
                S.op("vector", lambda e, s=s, j=j: e.tensor_mul(
                    hstg[s], hstg[s], G2bc4[:, 512 * j:512 * (j + 1)]))
                tSqH = S.op("vector", lambda e, tc=tc, j=j, s=s: e.reduce_max(
                    hmx[:, 11 * tc + j:11 * tc + j + 1], hstg[s], AX,
                    apply_absolute_value=True), inc="sV")
                tEvG[(j, tc)] = tSqH
                S.wait("gpsimd", tSqH)
                tSpH[s] = S.dma("gpsimd",
                                sh_h[128 * tc:128 * (tc + 1),
                                     512 * j:512 * (j + 1)], hstg[s], f"hsp{s}")
                target = (64 * (j * NT + tc + 1)) // (NI * NT)
                while tern_it is not None and tern_cnt[0] < target:
                    if next(tern_it, None) is None:
                        tern_it = None
                        break
                    tern_cnt[0] += 1
        drain(tern_it)
        tern_it = None

    if "dw" in phases and "p4" not in phases:
        drain(tern_dw_gen({2: None, 3: None}))

    S.barrier()

    hldq = [buf(f"hldq{i}", [128, 1408], F32, 158 + 5.5 * i, "p5m")
            for i in range(2)]
    uq2 = [buf(f"uq{i}", [128, 1408], F32, 169 + 5.5 * i, "p5m")
           for i in range(2)]
    sgq = buf("sgq", [128, 1408], F32, 180, "p5m")
    hqstq = [buf(f"hqstq{i}", [128, 1408], BF16, 185.5 + 2.75 * i, "p5m")
             for i in range(2)]

    spHq = {}
    inv2tok = {}

    def p4b_gen():
        tTrU = [None, None]
        tEvB = {6: None, 7: None}
        tSpSt = [None, None]
        tFreeLd = [None, None]
        tPend = [None]
        qidx = 0
        for tc in range(NT):
            tSum2 = vsmall(lambda e, tc=tc: e.reduce_sum(
                lnc[:, 16:17], hss[:, 11 * tc:11 * (tc + 1)], AX))
            S.wait("scalar", tSum2)
            tRs2 = S.op("scalar", lambda e: e.activation(
                lnc[:, 17:18], lnc[:, 16:17], ACT.Sqrt, scale=1.0 / INTER,
                bias=sm[:, 9:10]), inc="sS")
            S.wait("vector", tRs2)
            vsmall(lambda e: e.reciprocal(lnc[:, 18:19], lnc[:, 17:18]))
            vsmall(lambda e, tc=tc: e.reduce_max(
                lnc[:, 19:20], hmx[:, 11 * tc:11 * (tc + 1)], AX,
                apply_absolute_value=True))
            vsmall(lambda e: e.tensor_scalar_mul(
                lnc[:, 20:21], lnc[:, 19:20], lnc[:, 18:19]))
            tClip = vsmall(lambda e: e.tensor_single_scalar(
                lnc[:, 20:21], lnc[:, 20:21], 1e-5, ALU.max))
            vsmall(lambda e: e.reciprocal(lnc[:, 21:22], lnc[:, 20:21]))
            vsmall(lambda e: e.tensor_scalar_mul(
                lnc[:, 21:22], lnc[:, 21:22], 127.0))
            vsmall(lambda e: e.tensor_scalar_mul(
                lnc[:, 22:23], lnc[:, 21:22], lnc[:, 18:19]))
            S.wait("scalar", tClip)
            inv2tok[tc] = S.op("scalar", lambda e, tc=tc: e.activation(
                invsx2[:, tc:tc + 1], lnc[:, 20:21], ACT.Identity,
                scale=sm[:, 1:2]), inc="sS")
            for q in range(4):
                b = qidx % 2
                if tPend[0] is not None:
                    tPend[0]()
                    tPend[0] = None
                S.wait("sync", tFreeLd[b])
                tLd = S.dma("sync", hldq[b],
                            sh_h[128 * tc:128 * (tc + 1),
                                 1408 * q:1408 * (q + 1)], f"hldq{b}")
                S.wait("vector", tLd)
                tV1 = S.op("vector", lambda e, b=b: e.tensor_scalar(
                    hldq[b], hldq[b], lnc[:, 22:23], 128.5,
                    ALU.mult, ALU.add), inc="sV")
                emit_ladder("vector", hldq[b], uq2[b], sgq, 1408,
                            uguard=(tV1, tTrU[b]))
                tHq = S.op("vector", lambda e, b=b: e.scalar_tensor_tensor(
                    uq2[b], hldq[b], -0.5, uq2[b], ALU.add, ALU.subtract),
                    inc="sV")
                tFreeLd[b] = tHq

                def mk_pend(tc=tc, q=q, b=b, tHq=tHq, qi=qidx):
                    def emitp():
                        tTr = None
                        for sub in range(3):
                            jmax = 4 if sub < 2 else 3
                            bq = 6 + ((qi * 3 + sub) % 2)
                            S.wait("tensor", tHq if sub == 0 else None,
                                   tEvB[bq])
                            for jj in range(jmax):
                                c = 4 * sub + jj
                                pb = PS[:, 512 * bq + 128 * jj:
                                        512 * bq + 128 * (jj + 1)]
                                tTr = S.op(
                                    "tensor", lambda e, pb=pb, b=b, c=c:
                                    e.transpose(
                                        pb, uq2[b][:, 128 * c:128 * (c + 1)],
                                        ident),
                                    inc="sT" if jj == jmax - 1 else None)
                            S.wait("scalar", tTr,
                                   tSpSt[b] if sub == 0 else None)
                            tEvB[bq] = S.op(
                                "scalar",
                                lambda e, b=b, sub=sub, bq=bq, jmax=jmax:
                                e.copy(hqstq[b][:, 512 * sub:
                                                512 * sub + 128 * jmax],
                                       bank(bq, 128 * jmax)), inc="sS")
                        tTrU[b] = tTr
                        S.wait("gpsimd", tEvB[6], tEvB[7])
                        dsth = shqT_h[1408 * q:1408 * (q + 1),
                                      128 * tc:128 * (tc + 1)].rearrange(
                            "(cc p) t -> p cc t", p=128)
                        tSpSt[b] = S.dma("gpsimd", dsth, hqstq[b],
                                         f"hqstq{b}")
                        spHq[(tc, q)] = tSpSt[b]
                    return emitp

                tPend[0] = mk_pend()
                qidx += 1
                yield (tc, q)
        if tPend[0] is not None:
            tPend[0]()
            tPend[0] = None

    p4b_iter = None
    if "p4b" in phases:
        p4b_iter = p4b_gen()
        for _ in range(16):
            next(p4b_iter)
        if "p5" not in phases:
            drain(p4b_iter)
            p4b_iter = None
            S.barrier()

    hqsc = [buf(f"hqsc{i}", [128, NI * 512], BF16, 0 + 11 * i, "p5m")
            for i in range(2)]
    dwq = [buf(f"dwq{i}", [128, NI * 2048], BF16, 22 + 44 * i, "p5m")
           for i in range(2)]
    x2part = buf("x2part", [128, 4 * 2048], F32, 110, "p5m")
    x1nld = buf("x1nld", [128, D], F32, 142, "p5m")
    outst = buf("outst", [128, D], F32, 150, "p5m")

    if "p5" in phases:
        tDw = [None, None]
        tHqLd = [None, None]
        tMM5 = {}
        tEv5 = {}
        tX1l = {}
        tSttX1 = {}
        tOutDma = None
        for sc in range(4):
            for qtr in range(4):
                bq = qtr % 2
                S.wait("sync", tMM5.get((sc, qtr - 2, 3, 3)),
                       tMM5.get((sc - 1, qtr + 2, 3, 3)),
                       *[spHq[(4 * sc + i, qtr)] for i in range(4)])
                hsrc = shqT_h[1408 * qtr:1408 * (qtr + 1),
                              512 * sc:512 * (sc + 1)].rearrange(
                    "(cc p) t -> p cc t", p=128)
                tHqLd[bq] = S.dma("sync", hqsc[bq], hsrc, f"hqsc{bq}")
                tDw[bq] = S.dma("sync", dwq[bq],
                                sdwT_h[1408 * qtr:1408 * (qtr + 1), :].rearrange(
                                    "(cc p) f -> p cc f", p=128), f"dwq{bq}")
                for ts in range(4):
                    tglob = 4 * sc + ts
                    if qtr == 3:
                        S.wait("sync", tSttX1.get((sc, ts - 1)) if ts > 0
                               else tSttX1.get((sc - 1, 3)))
                        tX1l[(sc, ts)] = S.dma(
                            "sync", x1nld,
                            sx1n_h[128 * tglob:128 * (tglob + 1), :], "x1nld")
                    S.wait("tensor", tHqLd[bq], tDw[bq],
                           *[tEv5.get((sc, qtr, ts - 1, f4))
                             for f4 in range(4)],
                           *([tEv5.get((sc, qtr - 1, 3, f4))
                              for f4 in range(4)] if ts == 0 else []),
                           *([tEv5.get((sc - 1, 3, 3, f4))
                              for f4 in range(4)]
                             if (ts == 0 and qtr == 0) else []))
                    for c2 in range(NI):
                        for f4 in range(4):
                            tMM5[(sc, qtr, ts, f4)] = S.op(
                                "tensor",
                                lambda e, bq=bq, ts=ts, c2=c2, f4=f4:
                                e.matmul(
                                    bank(f4),
                                    hqsc[bq][:, 512 * c2 + 128 * ts:
                                             512 * c2 + 128 * (ts + 1)],
                                    dwq[bq][:, 2048 * c2 + 512 * f4:
                                            2048 * c2 + 512 * (f4 + 1)],
                                    start=(c2 == 0), stop=(c2 == NI - 1)),
                                inc="sT" if c2 == NI - 1 else None)
                    for f4 in range(4):
                        xp4 = x2part[:, 2048 * ts + 512 * f4:
                                     2048 * ts + 512 * (f4 + 1)]
                        if qtr == 0:
                            S.wait("scalar", tMM5[(sc, qtr, ts, f4)])
                            tEv5[(sc, qtr, ts, f4)] = S.op(
                                "scalar", lambda e, xp4=xp4, f4=f4:
                                e.copy(xp4, bank(f4)), inc="sS")
                        else:
                            S.wait("vector", tMM5[(sc, qtr, ts, f4)],
                                   tEv5.get((sc, 0, ts, f4))
                                   if qtr == 1 else None)
                            tEv5[(sc, qtr, ts, f4)] = S.op(
                                "vector", lambda e, xp4=xp4, f4=f4:
                                e.tensor_add(xp4, xp4, bank(f4)), inc="sV")
                    if qtr == 3:
                        xpt = x2part[:, 2048 * ts:2048 * (ts + 1)]
                        S.wait("vector", tX1l[(sc, ts)], tOutDma,
                               inv2tok[tglob])
                        tStt = S.op(
                            "vector", lambda e, xpt=xpt, tglob=tglob:
                            e.scalar_tensor_tensor(
                                outst, xpt, invsx2[:, tglob:tglob + 1],
                                x1nld, ALU.mult, ALU.add), inc="sV")
                        tSttX1[(sc, ts)] = tStt

                        def ss(fn):
                            tok = S.op("scalar", fn, inc="sS")
                            S.wait("scalar", tok)
                            return tok
                        S.wait("scalar", tStt)
                        ss(lambda e, xpt=xpt: e.activation(
                            xpt, outst, ACT.Identity,
                            accum_out=lnc[:, 0:1]))
                        ss(lambda e: e.activation(
                            lnc[:, 2:3], lnc[:, 0:1], ACT.Identity,
                            scale=-1.0 / 2048.0))
                        ss(lambda e: e.activation(
                            outst, outst, ACT.Identity,
                            bias=lnc[:, 2:3]))
                        ss(lambda e, xpt=xpt: e.activation(
                            xpt, outst, ACT.Square, accum_out=lnc[:, 4:5]))
                        tSq5 = ss(lambda e: e.activation(
                            lnc[:, 5:6], lnc[:, 4:5], ACT.Sqrt,
                            scale=1.0 / 2048.0, bias=sm[:, 9:10]))
                        S.wait("vector", tSq5)
                        tRc5 = vsmall(lambda e: e.reciprocal(
                            lnc[:, 6:7], lnc[:, 5:6]))
                        S.wait("scalar", tRc5)
                        tFin = ss(lambda e: e.activation(
                            outst, outst, ACT.Identity, scale=lnc[:, 6:7]))
                        S.wait("gpsimd", tFin)
                        tOutDma = S.dma(
                            "gpsimd", out_h[128 * tglob:128 * (tglob + 1), :],
                            outst, "outsp")
                    if p4b_iter is not None:
                        next(p4b_iter, None)

    S.barrier()
    S.build()
    return nc, S, locals()




from concourse.bass_utils import run_bass_kernel_spmd

_PROG_CACHE = {}


def kernel(L0_attn_x, L0_attn_wq, L0_attn_wk, L0_attn_wv, L0_attn_wo,
           L0_ffn_gate_w, L0_ffn_down_w, L0_ffn_gate_norm_w,
           L0_ffn_down_norm_w):
    tabs = host_tables()
    if "nc" not in _PROG_CACHE:
        nc, _S, _env = build_program(dbg=(), phases=ALLP)
        _PROG_CACHE["nc"] = nc
    nc = _PROG_CACHE["nc"]
    base = {
        "wq": np.ascontiguousarray(L0_attn_wq),
        "wk": np.ascontiguousarray(L0_attn_wk),
        "wv": np.ascontiguousarray(L0_attn_wv),
        "wo": np.ascontiguousarray(L0_attn_wo),
        "gw": np.ascontiguousarray(L0_ffn_gate_w),
        "dwm": np.ascontiguousarray(L0_ffn_down_w),
        "g1": np.ascontiguousarray(L0_ffn_gate_norm_w).reshape(1, -1),
        "g2": np.ascontiguousarray(L0_ffn_down_norm_w).reshape(1, -1),
        "cc2": tabs["ccP"], "ss2": tabs["ssP"],
        "identc": tabs["ident"], "onesc": tabs["ones128"],
    }
    in_maps = []
    for b in range(8):
        m = dict(base)
        m["x"] = np.ascontiguousarray(L0_attn_x[b])
        in_maps.append(m)
    res = run_bass_kernel_spmd(nc, in_maps, core_ids=list(range(8)))
    return np.stack([r["out"] for r in res.results], axis=0)
